# version 05
# baseline (speedup 1.0000x reference)
import numpy as np
from contextlib import ExitStack

import concourse.bass as bass
import concourse.mybir as mybir
from concourse.bass import AP
from concourse.bass_utils import run_bass_kernel_spmd

F32 = mybir.dt.float32
BF16 = mybir.dt.bfloat16
AF = mybir.ActivationFunctionType
ALU = mybir.AluOpType
AX = mybir.AxisListType

D = 1024
SEQ = 4096
HEADS = 8
TILE = 256
CH = 128
NCH = TILE // CH
EPS = 1e-5

ENGINES = ("tensor", "vector", "scalar", "sync", "gpsimd")
DMA_Q = ("sync", "gpsimd")
GRANULE = {"wzB": 2048, "wxB": 4096, "woB": 2048, "v1B": 512, "u2B": 512,
           "u1B": 512, "arow": 1024}
PASSES = ((0, 0), (0, 1), (1, 0))


def build_nc(n_tiles=16):
    nc = bass.Bass("TRN2", target_bir_lowering=False)
    ctx = ExitStack()

    dx = nc.dram_tensor("x", [SEQ, D], F32, kind="ExternalInput")
    dout = nc.dram_tensor("out", [SEQ, D], F32, kind="ExternalOutput")
    dwz = nc.dram_tensor("wz", [16, 128, 8, 128], F32, kind="ExternalInput")
    dwx = nc.dram_tensor("wx", [8, 128, 2048], F32, kind="ExternalInput")
    dwbc = nc.dram_tensor("wbc", [128, 8, 128], F32, kind="ExternalInput")
    dwdt = nc.dram_tensor("wdt", [128, 8, 8], F32, kind="ExternalInput")
    dwo = nc.dram_tensor("wo", [16, 128, 1024], F32, kind="ExternalInput")
    du1 = nc.dram_tensor("u1", [8, 128, 256], F32, kind="ExternalInput")
    dv1 = nc.dram_tensor("v1", [32, 128, 2, 128], F32, kind="ExternalInput")
    du2 = nc.dram_tensor("u2", [32, 128, 256], F32, kind="ExternalInput")
    dv2 = nc.dram_tensor("v2", [128, 2, 1024], F32, kind="ExternalInput")
    dbz = nc.dram_tensor("bz", [128, 16], F32, kind="ExternalInput")
    dbx = nc.dram_tensor("bx", [1, 2048], F32, kind="ExternalInput")
    dbB = nc.dram_tensor("bB", [64, 1], F32, kind="ExternalInput")
    dbC = nc.dram_tensor("bC", [64, 1], F32, kind="ExternalInput")
    dbdt = nc.dram_tensor("bdt", [2, 8], F32, kind="ExternalInput")
    dbo = nc.dram_tensor("bo", [1, 1024], F32, kind="ExternalInput")
    db1 = nc.dram_tensor("b1", [128, 32], F32, kind="ExternalInput")
    db2 = nc.dram_tensor("b2", [1, 1024], F32, kind="ExternalInput")
    dgam = nc.dram_tensor("gam", [128, 8], F32, kind="ExternalInput")
    dbet = nc.dram_tensor("bet", [128, 8], F32, kind="ExternalInput")
    dalog = nc.dram_tensor("alog", [1, 8], F32, kind="ExternalInput")
    ddsk = nc.dram_tensor("dsk", [1, 8], F32, kind="ExternalInput")
    dident = nc.dram_tensor("ident", [128, 128], F32, kind="ExternalInput")
    dtri = nc.dram_tensor("tri", [128, 128], F32, kind="ExternalInput")
    dones = nc.dram_tensor("onesc", [128, 128], F32, kind="ExternalInput")
    dwzHL = nc.dram_tensor("wzHL", [16, 2, 128, 8, 128], BF16, kind="Internal")
    dwxHL = nc.dram_tensor("wxHL", [8, 2, 128, 2048], BF16, kind="Internal")
    dwoHL = nc.dram_tensor("woHL", [16, 2, 128, 1024], BF16, kind="Internal")
    du1HL = nc.dram_tensor("u1HL", [8, 2, 128, 256], BF16, kind="Internal")
    dv1HL = nc.dram_tensor("v1HL", [32, 2, 128, 2, 128], BF16, kind="Internal")
    du2HL = nc.dram_tensor("u2HL", [32, 2, 128, 256], BF16, kind="Internal")

    sb = lambda name, p, f: ctx.enter_context(nc.sbuf_tensor(name, [p, f], F32))
    sb3 = lambda name, p, a, b: ctx.enter_context(nc.sbuf_tensor(name, [p, a, b], F32))
    sb4 = lambda name, p, a, b, c: ctx.enter_context(
        nc.sbuf_tensor(name, [p, a, b, c], F32))

    xT = [sb3("xT0", 128, NCH, 1024), sb3("xT1", 128, NCH, 1024)]
    uT = sb3("uT", 128, 8, TILE)
    szT = sb3("szT", 128, 16, TILE)
    xsT = sb("xsT", 128, 2048)
    BfT = sb("BfT", 64, TILE)
    CfT = sb("CfT", 64, TILE)
    dtrT = sb("dtrT", 128, NCH * 8)
    dtT = sb("dtT", 128, NCH * 8)
    dtfmT = sb("dtfmT", 8, 128)
    ldT = sb("ldT", 8, 128)
    w8T = sb("w8T", 8, 128)
    aT = sb("aT", 8, 128)
    wdt8T = sb("wdt8T", 8, 128)
    expaT = sb("expaT", 8, 128)
    atmT = sb("atmT", 128, 16)
    arow = sb("arow", 1, 2048)
    e8cT = sb("e8cT", 64, 8)
    BtmT = sb("BtmT", 128, 64)
    BwT = sb("BwT", 128, 512)
    EminT = sb("EminT", 128, 1024)
    GTmT = sb("GTmT", 128, 128)
    MTT = sb("MTT", 128, 1024)
    MTvT = sb("MTvT", 64, 1024)
    hS = sb("hS", 64, 2048)
    rT = sb3("rT", 128, NCH, 1024)
    x1tmT = sb3("x1tm", 128, NCH, 1024)
    HsT = [sb("Hs0", 128, TILE), sb("Hs1", 128, TILE)]
    QT = sb3("QT", 128, 2, TILE)
    statT = sb("statT", 128, 48)
    rowT = sb("rowT", 1, 1024)
    DgT = sb("DgT", 128, 768)
    DdT = sb("DdT", 128, 1024)
    DbcT = sb("DbcT", 128, 8)
    AcolT = sb("AcolT", 8, 1)
    ArowT = sb("ArowT", 1, 8)

    sbb = lambda name, shape: ctx.enter_context(nc.sbuf_tensor(name, shape, BF16))
    wzB = sbb("wzB", [128, 2, 2, 8, 128])
    wxB = sbb("wxB", [128, 2, 2, 2048])
    woB = sbb("woB", [128, 2, 2, 1024])
    v1B = sbb("v1B", [128, 2, 2, 2, 128])
    u2B = sbb("u2B", [128, 2, 2, 256])
    u1B = sbb("u1B", [128, 2, 2, 256])
    u16 = sbb("u16", [128, 2, 8, 256])
    g16 = sbb("g16", [128, 2, 2048])
    P16 = sbb("P16", [128, 2, 2, 256])
    H16 = [sbb("H16a", [128, 2, 256]), sbb("H16b", [128, 2, 256])]
    v2R = sb3("v2R", 128, 2, 1024)
    wbcR = sb3("wbcR", 128, 8, 128)
    wdtR = sb3("wdtR", 128, 8, 8)
    bzR = sb("bzR", 128, 16)
    bxR = sb("bxR", 1, 2048)
    bBR = sb("bBR", 64, 1)
    bCR = sb("bCR", 64, 1)
    bdtR = sb("bdtR", 2, 8)
    boR = sb("boR", 1, 1024)
    b1R = sb("b1R", 128, 32)
    b2R = sb("b2R", 1, 1024)
    gamR = sb("gamR", 128, 8)
    betR = sb("betR", 128, 8)
    alogR = sb("alogR", 1, 8)
    dskR = sb("dskR", 1, 8)
    identR = sb("identR", 128, 128)
    triR = sb("triR", 128, 128)
    onesR = sb("onesR", 128, 128)

    PS = ctx.enter_context(nc.psum_tensor("PS", [128, 4096], F32))

    def ps(lo, n, p0=0, pn=128):
        return PS[p0:p0 + pn, lo:lo + n]

    class SemC:
        def __init__(self, name):
            self.sem = ctx.enter_context(nc.semaphore(name))
            self.n = 0

    sPE, sV, sS = SemC("sPE"), SemC("sV"), SemC("sS")
    dX = [SemC("dXa"), SemC("dXb")]
    dWZ = [SemC("dWZa"), SemC("dWZb")]
    dWX = [SemC("dWXa"), SemC("dWXb")]
    dWO = [SemC("dWOa"), SemC("dWOb")]
    dU1 = [SemC("dU1a"), SemC("dU1b")]
    dV1 = [SemC("dV1a"), SemC("dV1b")]
    dU2 = [SemC("dU2a"), SemC("dU2b")]
    dRES, dOUT, dPR = SemC("dRES"), SemC("dOUT"), SemC("dPR")
    dROW = [SemC("dROWa"), SemC("dROWb")]
    OWN = {"tensor": sPE, "vector": sV, "scalar": sS}

    blocks = []
    cur = {}
    dirty = {"vector": set(), "scalar": set()}
    waited = {}
    lastw = {}
    readers = {}

    def new_block():
        nonlocal cur
        cur = {e: [] for e in ENGINES}
        blocks.append(cur)
        for s in dirty.values():
            s.clear()

    def keys_of(ap):
        t = getattr(ap, "tensor", ap)
        name = getattr(t, "name", None)
        off = getattr(ap, "offset", 0)
        if name == "PS":
            span = 1
            for stp, cnt in list(ap.ap)[1:]:
                span += (cnt - 1) * stp
            return [("PS", b) for b in range(off // 512, (off + span - 1) // 512 + 1)]
        g = GRANULE.get(name)
        if g is not None:
            return [(name, off // g)]
        return [name]

    def step(eng, emit, rd=(), wr=(), extra=(), inc=None, inc_amt=1):
        rdk = [k for a in rd for k in keys_of(a)]
        wrk = [k for a in wr for k in keys_of(a)]
        want = {}

        def need(dep):
            deng, dsem, dval = dep
            if deng == eng and deng not in DMA_Q:
                return
            cur_ = want.get(id(dsem))
            if cur_ is None or cur_[1] < dval:
                want[id(dsem)] = (dsem, dval)

        for k in rdk:
            w = lastw.get(k)
            if w:
                need(w)
        for k in wrk:
            w = lastw.get(k)
            if w:
                need(w)
            for dep in readers.get(k, {}).values():
                need(dep)
        for (s, v) in extra:
            need(("_x", s, v))
        if inc is None:
            inc = OWN.get(eng)
        inc.n += inc_amt
        me = (eng, inc, inc.n)
        for k in rdk:
            readers.setdefault(k, {})[eng] = me
        for k in wrk:
            lastw[k] = me
            readers[k] = {}
        nd = False
        if eng in dirty:
            if (set(rdk) | set(wrk)) & dirty[eng]:
                nd = True
                dirty[eng].clear()
            dirty[eng].update(wrk)
        wl = []
        for (dsem, dval) in want.values():
            key = (eng, id(dsem))
            if waited.get(key, -1) < dval:
                waited[key] = dval
                wl.append((dsem.sem, dval))

        def fn(e, wl=wl, emit=emit, inc=inc, amt=inc_amt, nd=nd):
            if nd:
                e.drain()
            for (sem, v) in wl:
                e.wait_ge(sem, v)
            ins = emit(e)
            if ins is not None:
                ins.then_inc(inc.sem, amt)
            return ins

        cur[eng].append(fn)

    def pe(emit, rd=(), wr=()):
        step("tensor", emit, rd=rd, wr=wr)

    def ve(emit, rd=(), wr=()):
        step("vector", emit, rd=rd, wr=wr)

    def sc(emit, rd=(), wr=()):
        step("scalar", emit, rd=rd, wr=wr)

    def dma(queue, out_ap, in_ap, sem):
        step(queue, lambda e, o=out_ap, i=in_ap: e.dma_start(out=o, in_=i),
             rd=[in_ap], wr=[out_ap], inc=sem, inc_amt=16)

    def setw(buf_ap, sem):
        for k in keys_of(buf_ap):
            lastw[k] = ("sync", sem, sem.n)

    def x_hbm(j):
        return AP(dx, j * TILE * D, [[D, 128], [CH * D, NCH], [1, D]])

    def wz_hbm(ft):
        return AP(dwz, ft * 128 * 8 * 128, [[1024, 128], [128, 8], [1, 128]])

    def wx_hbm(k):
        return AP(dwx, k * 128 * 2048, [[2048, 128], [1, 2048]])

    def wo_hbm(k):
        return AP(dwo, k * 128 * 1024, [[1024, 128], [1, 1024]])

    def v1_hbm(m):
        return AP(dv1, m * 128 * 2 * 128, [[256, 128], [128, 2], [1, 128]])

    def u2_hbm(m):
        return AP(du2, m * 128 * 256, [[256, 128], [1, 256]])

    def u1_hbm(k):
        return AP(du1, k * 128 * 256, [[256, 128], [1, 256]])

    def hl_hbm(dten, blk, free):
        base = blk * 2 * 128 * free
        return AP(dten, base, [[free, 128], [128 * free, 2], [1, free]])

    def wzHL_hbm(ft):
        return hl_hbm(dwzHL, ft, 1024)

    def wxHL_hbm(k):
        return hl_hbm(dwxHL, k, 2048)

    def woHL_hbm(k):
        return hl_hbm(dwoHL, k, 1024)

    def u1HL_hbm(k):
        return hl_hbm(du1HL, k, 256)

    def v1HL_hbm(m):
        return hl_hbm(dv1HL, m, 256)

    def u2HL_hbm(m):
        return hl_hbm(du2HL, m, 256)

    RSTD1, NEGM1 = 0, 4
    SSUM, SSSQ, SVAR, SLNV, SMR = 8, 10, 12, 14, 16
    RSTD2, NEGM2, RSTD3, NEGM3 = 28, 30, 32, 34
    SPE8 = 36
    R1ROW, R2ROW, R3ROW = 0, 512, 768

    def ln_stats(src):
        for c in range(NCH):
            ve(lambda e, c=c, src=src: e.reduce_sum(
                statT[:, SSUM + c:SSUM + c + 1], src(c), axis=AX.X),
               rd=[src(c)], wr=[statT])
            sc(lambda e, c=c, src=src: e.activation(
                EminT[:], src(c), AF.Square,
                accum_out=statT[:, SSSQ + c:SSSQ + c + 1]),
               rd=[src(c)], wr=[EminT, statT])

    def ln_rstd(rcol, ncol):
        ve(lambda e: e.tensor_scalar_mul(
            statT[:, SMR:SMR + 2], statT[:, SSUM:SSUM + 2], 1.0 / D),
           rd=[statT], wr=[statT])
        ve(lambda e: e.tensor_scalar_mul(
            statT[:, SVAR:SVAR + 2], statT[:, SSSQ:SSSQ + 2], 1.0 / D),
           rd=[statT], wr=[statT])
        ve(lambda e: e.tensor_mul(statT[:, SLNV:SLNV + 2], statT[:, SMR:SMR + 2],
                                  statT[:, SMR:SMR + 2]), rd=[statT], wr=[statT])
        ve(lambda e: e.tensor_sub(statT[:, SVAR:SVAR + 2], statT[:, SVAR:SVAR + 2],
                                  statT[:, SLNV:SLNV + 2]), rd=[statT], wr=[statT])
        sc(lambda e: e.activation(statT[:, SLNV:SLNV + 2], statT[:, SVAR:SVAR + 2],
                                  AF.Ln, bias=statT[:, 47:48]),
           rd=[statT], wr=[statT])
        sc(lambda e, rcol=rcol: e.activation(
            statT[:, rcol:rcol + 2], statT[:, SLNV:SLNV + 2], AF.Exp, scale=-0.5),
           rd=[statT], wr=[statT])
        ve(lambda e, rcol=rcol: e.tensor_mul(
            statT[:, SMR:SMR + 2], statT[:, SMR:SMR + 2], statT[:, rcol:rcol + 2]),
           rd=[statT], wr=[statT])
        ve(lambda e, ncol=ncol: e.tensor_scalar_mul(
            statT[:, ncol:ncol + 2], statT[:, SMR:SMR + 2], -1.0),
           rd=[statT], wr=[statT])

    def negmr_row(col, row_off):
        pe(lambda e, col=col: e.transpose(ps(3968, 128, 0, 1),
                                          statT[:, col:col + 1], identR[:]),
           rd=[statT, identR], wr=[ps(3968, 128)])
        ve(lambda e, ro=row_off: e.tensor_copy(
            AP(rowT, ro, [[1024, 1], [1, 128]]), ps(3968, 128, 0, 1)),
           rd=[ps(3968, 128)], wr=[rowT])

    def dg_build(slot, col):
        ve(lambda e, slot=slot, col=col: e.tensor_scalar_mul(
            DgT[:, slot * 128:(slot + 1) * 128], identR[:], statT[:, col:col + 1]),
           rd=[identR, statT], wr=[DgT])

    def row_ap(off):
        return AP(rowT, off, [[1024, 1], [1, 128]])

    new_block()
    res_list = [
        (v2R[:], dv2.ap()), (wbcR[:], dwbc.ap()),
        (wdtR[:], dwdt.ap()), (bzR[:], dbz.ap()), (bxR[:], dbx.ap()),
        (bBR[:], dbB.ap()), (bCR[:], dbC.ap()), (bdtR[:], dbdt.ap()),
        (boR[:], dbo.ap()), (b1R[:], db1.ap()), (b2R[:], db2.ap()),
        (gamR[:], dgam.ap()), (betR[:], dbet.ap()), (alogR[:], dalog.ap()),
        (dskR[:], ddsk.ap()), (identR[:], dident.ap()), (triR[:], dtri.ap()),
        (onesR[:], dones.ap()),
    ]
    for (dst, src) in res_list:
        dma("sync", dst, src, dRES)
    for (dst, src) in res_list:
        setw(dst, dRES)

    def presplit(src_dram, src_stride, hl_dram, nblocks, free):
        for i in range(nblocks):
            dma("sync", xsT[:, 0:free],
                AP(src_dram, i * src_stride, [[free, 128], [1, free]]), dPR)
            ve(lambda e, free=free: e.tensor_copy(g16[:, 0, 0:free],
                                                  xsT[:, 0:free]),
               rd=[xsT], wr=[g16])
            ve(lambda e, free=free: e.tensor_sub(g16[:, 1, 0:free],
                                                 xsT[:, 0:free],
                                                 g16[:, 0, 0:free]),
               rd=[xsT, g16], wr=[g16])
            for hl in range(2):
                dma("sync",
                    AP(hl_dram, (i * 2 + hl) * 128 * free,
                       [[free, 128], [1, free]]),
                    g16[:, hl, 0:free], dPR)

    presplit(dwz, 131072, dwzHL, 16, 1024)
    presplit(dwx, 262144, dwxHL, 8, 2048)
    presplit(dwo, 131072, dwoHL, 16, 1024)
    presplit(du1, 32768, du1HL, 8, 256)
    presplit(dv1, 32768, dv1HL, 32, 256)
    presplit(du2, 32768, du2HL, 32, 256)
    for hl_dram in (dwzHL, dwxHL, dwoHL, du1HL, dv1HL, du2HL):
        setw(hl_dram, dPR)

    dma("sync", xT[0][:], x_hbm(0), dX[0])
    if n_tiles > 1:
        dma("sync", xT[1][:], x_hbm(1), dX[1])
    dma("sync", wzB[:, 0], wzHL_hbm(0), dWZ[0])
    dma("sync", wzB[:, 1], wzHL_hbm(1), dWZ[1])
    dma("gpsimd", wxB[:, 0], wxHL_hbm(0), dWX[0])
    dma("gpsimd", wxB[:, 1], wxHL_hbm(1), dWX[1])
    dma("sync", woB[:, 0], woHL_hbm(0), dWO[0])
    dma("sync", woB[:, 1], woHL_hbm(1), dWO[1])
    dma("gpsimd", u1B[:, 0], u1HL_hbm(0), dU1[0])
    dma("gpsimd", u1B[:, 1], u1HL_hbm(1), dU1[1])
    dma("gpsimd", v1B[:, 0], v1HL_hbm(0), dV1[0])
    dma("gpsimd", v1B[:, 1], v1HL_hbm(1), dV1[1])
    dma("gpsimd", u2B[:, 0], u2HL_hbm(0), dU2[0])
    dma("gpsimd", u2B[:, 1], u2HL_hbm(1), dU2[1])

    sc(lambda e: e.activation(ArowT[:], alogR[:], AF.Exp),
       rd=[alogR], wr=[ArowT])
    ve(lambda e: e.tensor_scalar_mul(ArowT[:], ArowT[:], -1.0),
       rd=[ArowT], wr=[ArowT])
    pe(lambda e: e.transpose(ps(0, 1, 0, 8), ArowT[:], identR[0:1, 0:1]),
       rd=[ArowT, identR], wr=[ps(0, 1)])
    ve(lambda e: e.tensor_copy(AcolT[:], ps(0, 1, 0, 8)),
       rd=[ps(0, 1)], wr=[AcolT])
    pe(lambda e: e.matmul(ps(512, 8), onesR[0:1, 0:128], dskR[:],
                          start=True, stop=True),
       rd=[onesR, dskR], wr=[ps(512, 8)])
    ve(lambda e: e.tensor_copy(DbcT[:], ps(512, 8)), rd=[ps(512, 8)], wr=[DbcT])
    for h in range(HEADS):
        ve(lambda e, h=h: e.tensor_scalar_mul(DdT[:, h * 128:(h + 1) * 128],
                                              identR[:], DbcT[:, h:h + 1]),
           rd=[identR, DbcT], wr=[DdT])
    ve(lambda e: e.memset(hS[:], 0.0), wr=[hS])
    ve(lambda e: e.memset(statT[:, 47:48], EPS), wr=[statT])

    ln_stats(lambda c: xT[0][:, c, :])
    ln_rstd(RSTD1, NEGM1)
    for c in range(NCH):
        dg_build(c, RSTD1 + c)
        negmr_row(NEGM1 + c, R1ROW + c * 128)

    def ssd_chunk(c):
        sc(lambda e, c=c: e.activation(statT[:, SPE8:SPE8 + 8],
                                       dtrT[:, c * 8:(c + 1) * 8], AF.Exp),
           rd=[dtrT], wr=[statT])
        sc(lambda e, c=c: e.activation(dtT[:, c * 8:(c + 1) * 8],
                                       statT[:, SPE8:SPE8 + 8], AF.Ln,
                                       bias=onesR[:, 0:1]),
           rd=[statT, onesR], wr=[dtT])
        pe(lambda e, c=c: e.transpose(ps(1152, 128, 0, 8),
                                      dtT[:, c * 8:(c + 1) * 8], identR[:]),
           rd=[dtT, identR], wr=[ps(1152, 128)])
        ve(lambda e: e.tensor_copy(dtfmT[:], ps(1152, 128, 0, 8)),
           rd=[ps(1152, 128)], wr=[dtfmT])
        ve(lambda e: e.tensor_scalar_mul(ldT[:], dtfmT[:], AcolT[:]),
           rd=[dtfmT, AcolT], wr=[ldT])
        ve(lambda e: e.tensor_tensor_scan(aT[:], ldT[:], onesR[0:8, 0:128],
                                          0.0, ALU.add, ALU.mult),
           rd=[ldT, onesR], wr=[aT])
        sc(lambda e: e.activation(expaT[:], aT[:], AF.Exp), rd=[aT], wr=[expaT])
        sc(lambda e: e.activation(w8T[:], aT[:], AF.Exp, scale=-1.0,
                                  bias=aT[:, 127:128]), rd=[aT], wr=[w8T])
        ve(lambda e: e.tensor_mul(wdt8T[:], w8T[:], dtfmT[:]),
           rd=[w8T, dtfmT], wr=[wdt8T])
        pe(lambda e: e.transpose(ps(1280, 8), aT[:], identR[0:8, 0:8]),
           rd=[aT, identR], wr=[ps(1280, 8)])
        pe(lambda e: e.transpose(ps(1288, 8), wdt8T[:], identR[0:8, 0:8]),
           rd=[wdt8T, identR], wr=[ps(1288, 8)])
        ve(lambda e: e.tensor_copy(atmT[:], ps(1280, 16)),
           rd=[ps(1280, 16)], wr=[atmT])
        dma("sync", AP(arow, 0, [[2048, 1], [128, 8], [1, 128]]), aT[:], dROW[0])
        dma("sync", AP(arow, 1024, [[2048, 1], [128, 8], [1, 128]]), expaT[:],
            dROW[1])
        for q4 in range(2):
            pe(lambda e, q4=q4: e.matmul(
                ps(q4 * 512, 512), onesR[0:1, 0:128],
                AP(arow, q4 * 512, [[2048, 1], [1, 512]]), start=True, stop=True),
               rd=[onesR, arow], wr=[ps(q4 * 512, 512)])
        pe(lambda e, c=c: e.matmul(ps(1024, 128), BfT[:, c * 128:(c + 1) * 128],
                                   CfT[:, c * 128:(c + 1) * 128],
                                   start=True, stop=True),
           rd=[BfT, CfT], wr=[ps(1024, 128)])
        pe(lambda e, c=c: e.transpose(ps(1296, 64), BfT[:, c * 128:(c + 1) * 128],
                                      identR[0:64, 0:64]),
           rd=[BfT, identR], wr=[ps(1296, 64)])
        pe(lambda e: e.matmul(ps(1408, 8, 0, 64), onesR[0:1, 0:64],
                              AP(arow, 1024 + 127, [[2048, 1], [128, 8]]),
                              start=True, stop=True),
           rd=[onesR, AP(arow, 1024, [[2048, 1], [1, 1024]])], wr=[ps(1408, 8)])
        for h in range(HEADS):
            ve(lambda e, h=h: e.tensor_scalar(
                EminT[:, h * 128:(h + 1) * 128], ps(h * 128, 128),
                atmT[:, h:h + 1], 0.0, ALU.subtract, ALU.min),
               rd=[ps(h * 128, 128), atmT], wr=[EminT])
        ve(lambda e: e.tensor_mul(GTmT[:], ps(1024, 128), triR[:]),
           rd=[ps(1024, 128), triR], wr=[GTmT])
        ve(lambda e: e.tensor_copy(BtmT[:], ps(1296, 64)),
           rd=[ps(1296, 64)], wr=[BtmT])
        ve(lambda e: e.tensor_copy(e8cT[:], ps(1408, 8, 0, 64)),
           rd=[ps(1408, 8)], wr=[e8cT])
        sc(lambda e: e.activation(EminT[:], EminT[:], AF.Exp),
           rd=[EminT], wr=[EminT])
        for h in range(HEADS):
            ve(lambda e, h=h, c=c: e.scalar_tensor_tensor(
                MTT[:, h * 128:(h + 1) * 128], EminT[:, h * 128:(h + 1) * 128],
                dtT[:, c * 8 + h:c * 8 + h + 1], GTmT[:], ALU.mult, ALU.mult),
               rd=[EminT, dtT, GTmT], wr=[MTT])
        for h in range(HEADS):
            ve(lambda e, h=h: e.tensor_add(MTT[:, h * 128:(h + 1) * 128],
                                           MTT[:, h * 128:(h + 1) * 128],
                                           DdT[:, h * 128:(h + 1) * 128]),
               rd=[MTT, DdT], wr=[MTT])
        for h in range(HEADS):
            ve(lambda e, h=h: e.tensor_scalar_mul(
                BwT[:, h * 64:(h + 1) * 64], BtmT[:], atmT[:, 8 + h:9 + h]),
               rd=[BtmT, atmT], wr=[BwT])
        for q4 in range(2):
            pe(lambda e, q4=q4: e.matmul(
                ps(q4 * 512, 512, 0, 64), onesR[0:1, 0:64],
                AP(arow, 1024 + q4 * 512, [[2048, 1], [1, 512]]),
                start=True, stop=True),
               rd=[onesR, AP(arow, 1024, [[2048, 1], [1, 1024]])],
               wr=[ps(q4 * 512, 512)])
        for h in range(HEADS):
            ve(lambda e, h=h, c=c: e.tensor_mul(
                MTvT[:, h * 128:(h + 1) * 128], ps(h * 128, 128, 0, 64),
                CfT[:, c * 128:(c + 1) * 128]),
               rd=[ps(h * 128, 128), CfT], wr=[MTvT])
        for h in range(HEADS):
            for pt in range(2):
                off = 2048 + (2 * h + pt) * 128
                pe(lambda e, h=h, pt=pt, off=off: e.matmul(
                    ps(off, 128),
                    xsT[:, h * 256 + pt * 128:h * 256 + (pt + 1) * 128],
                    MTT[:, h * 128:(h + 1) * 128], start=True, stop=False),
                   rd=[xsT, MTT], wr=[ps(off, 128)])
                pe(lambda e, h=h, pt=pt, off=off: e.matmul(
                    ps(off, 128), hS[:, h * 256 + pt * 128:h * 256 + (pt + 1) * 128],
                    MTvT[:, h * 128:(h + 1) * 128], start=False, stop=True),
                   rd=[hS, MTvT], wr=[ps(off, 128)])
        gflat = x1tmT[:].rearrange("p a b -> p (a b)")
        ve(lambda e, c=c, gflat=gflat: e.tensor_tensor(
            gflat.rearrange("p (a b) -> p a b", a=16),
            ps(2048, 2048).rearrange("p (a b) -> p a b", a=16),
            szT[:, :, c * 128:(c + 1) * 128], ALU.mult),
           rd=[ps(2048, 2048), szT], wr=[x1tmT])
        ve(lambda e, gflat=gflat: e.tensor_copy(g16[:, 0, :], gflat),
           rd=[x1tmT], wr=[g16])
        ve(lambda e, gflat=gflat: e.tensor_sub(g16[:, 1, :], gflat, g16[:, 0, :]),
           rd=[x1tmT, g16], wr=[g16])
        for h in range(HEADS):
            pe(lambda e, h=h: e.matmul(
                ps(h * 256, 256, 0, 64), BwT[:, h * 64:(h + 1) * 64],
                xsT[:, h * 256:(h + 1) * 256], start=True, stop=True),
               rd=[BwT, xsT], wr=[ps(h * 256, 256)])
        for h in range(HEADS):
            ve(lambda e, h=h: e.scalar_tensor_tensor(
                hS[:, h * 256:(h + 1) * 256], hS[:, h * 256:(h + 1) * 256],
                e8cT[:, h:h + 1], ps(h * 256, 256, 0, 64), ALU.mult, ALU.add),
               rd=[hS, e8cT, ps(h * 256, 256)], wr=[hS])

    def xs_dt_chunk(j, c):
        for k in range(8):
            buf = k % 2
            for n4 in range(4):
                for pi, (wh, uh) in enumerate(PASSES):
                    st = (k == 0 and pi == 0)
                    pe(lambda e, k=k, c=c, n4=n4, buf=buf, wh=wh, uh=uh, st=st:
                       e.matmul(ps(n4 * 512, 512),
                                u16[:, uh, k, c * 128:(c + 1) * 128],
                                wxB[:, buf, wh, n4 * 512:(n4 + 1) * 512],
                                start=st, stop=False),
                       rd=[u16, wxB[:, buf]], wr=[ps(n4 * 512, 512)])
            nxt = 16 * j + 8 * c + k + 3
            if nxt <= 16 * n_tiles:
                dma("gpsimd", wxB[:, buf], wxHL_hbm((nxt - 1) % 8),
                    dWX[(nxt - 1) % 2])
        for n4 in range(4):
            pe(lambda e, n4=n4: e.matmul(
                ps(n4 * 512, 512), onesR[0:1, 0:128],
                bxR[0:1, n4 * 512:(n4 + 1) * 512], start=False, stop=True),
               rd=[onesR, bxR], wr=[ps(n4 * 512, 512)])
        ve(lambda e: e.tensor_copy(xsT[:], ps(0, 2048)),
           rd=[ps(0, 2048)], wr=[xsT])
        for k in range(8):
            pe(lambda e, c=c, k=k: e.matmul(
                ps(0, 8), uT[:, k, c * 128:(c + 1) * 128],
                wdtR[:, k, :], start=(k == 0), stop=False),
               rd=[uT, wdtR], wr=[ps(0, 8)])
        pe(lambda e: e.matmul(ps(0, 8), onesR[0:2, 0:128], bdtR[:],
                              start=False, stop=True),
           rd=[onesR, bdtR], wr=[ps(0, 8)])
        ve(lambda e, c=c: e.tensor_copy(dtrT[:, c * 8:(c + 1) * 8], ps(0, 8)),
           rd=[ps(0, 8)], wr=[dtrT])

    def outproj_chunk(j, c, xv):
        for k in range(16):
            buf = k % 2
            for n2 in range(2):
                for pi, (wh, gh) in enumerate(PASSES):
                    st = (k == 0 and pi == 0)
                    pe(lambda e, k=k, n2=n2, buf=buf, wh=wh, gh=gh, st=st:
                       e.matmul(
                        ps(2048 + n2 * 512, 512),
                        g16[:, gh, k * 128:(k + 1) * 128],
                        woB[:, buf, wh, n2 * 512:(n2 + 1) * 512],
                        start=st, stop=False),
                       rd=[g16, woB[:, buf]], wr=[ps(2048 + n2 * 512, 512)])
            nxt = 32 * j + 16 * c + k + 3
            if nxt <= 32 * n_tiles:
                dma("sync", woB[:, buf], woHL_hbm((nxt - 1) % 16),
                    dWO[(nxt - 1) % 2])
        for n2 in range(2):
            pe(lambda e, n2=n2: e.matmul(
                ps(2048 + n2 * 512, 512), onesR[0:1, 0:128],
                boR[0:1, n2 * 512:(n2 + 1) * 512], start=False, stop=True),
               rd=[onesR, boR], wr=[ps(2048 + n2 * 512, 512)])
        ve(lambda e, c=c, xv=xv: e.scalar_tensor_tensor(
            rT[:, c, :], xv[:, c, :], 2.0, ps(2048, 1024), ALU.mult, ALU.add),
           rd=[xv, ps(2048, 1024)], wr=[rT])

    for j in range(n_tiles):
        par = j % 2
        npar = (j + 1) % 2
        xv, xnv = xT[par], xT[npar]

        for c in range(NCH):
            base = c * 1024
            slot = 2 * par + c
            for f in range(8):
                pe(lambda e, c=c, f=f, base=base, slot=slot, xv=xv: e.matmul(
                    ps(base + f * 128, 128), xv[:, c, f * 128:(f + 1) * 128],
                    DgT[:, slot * 128:(slot + 1) * 128], start=True, stop=False),
                   rd=[xv, DgT], wr=[ps(base + f * 128, 128)])
                pe(lambda e, f=f, base=base, slot=slot: e.matmul(
                    ps(base + f * 128, 128), onesR[0:1, 0:128],
                    row_ap(R1ROW + slot * 128), start=False, stop=True),
                   rd=[onesR, rowT], wr=[ps(base + f * 128, 128)])
            for f in range(8):
                ve(lambda e, c=c, f=f, base=base: e.tensor_scalar(
                    uT[:, f, c * 128:(c + 1) * 128], ps(base + f * 128, 128),
                    gamR[:, f:f + 1], betR[:, f:f + 1], ALU.mult, ALU.add),
                   rd=[ps(base + f * 128, 128), gamR, betR], wr=[uT])
        ve(lambda e: e.tensor_copy(u16[:, 0].rearrange("p a b -> p (a b)"),
                                   uT[:].rearrange("p a b -> p (a b)")),
           rd=[uT], wr=[u16])
        ve(lambda e: e.tensor_sub(u16[:, 1].rearrange("p a b -> p (a b)"),
                                  uT[:].rearrange("p a b -> p (a b)"),
                                  u16[:, 0].rearrange("p a b -> p (a b)")),
           rd=[uT, u16], wr=[u16])
        for ft in range(18):
            bank = 4 + (ft % 4)
            if ft < 16:
                buf = ft % 2
                for k in range(8):
                    for pi, (wh, uh) in enumerate(PASSES):
                        st, sp = (k == 0 and pi == 0), (k == 7 and pi == 2)
                        pe(lambda e, k=k, buf=buf, bank=bank, wh=wh, uh=uh,
                           st=st, sp=sp:
                           e.matmul(ps(bank * 512, 256), wzB[:, buf, wh, k, :],
                                    u16[:, uh, k, :], start=st, stop=sp),
                           rd=[wzB[:, buf], u16], wr=[ps(bank * 512, 256)])
                sc(lambda e, ft=ft, bank=bank: e.activation(
                    szT[:, ft, :], ps(bank * 512, 256), AF.Silu,
                    bias=bzR[:, ft:ft + 1]),
                   rd=[ps(bank * 512, 256), bzR], wr=[szT])
                if ft + 2 < 16:
                    dma("sync", wzB[:, buf], wzHL_hbm(ft + 2), dWZ[ft % 2])
            elif ft == 16:
                for k in range(8):
                    pe(lambda e, k=k, bank=bank: e.matmul(
                        ps(bank * 512, 256, 0, 64), wbcR[:, k, 0:64], uT[:, k, :],
                        start=(k == 0), stop=(k == 7)),
                       rd=[wbcR, uT], wr=[ps(bank * 512, 256)])
                ve(lambda e, bank=bank: e.tensor_scalar_add(
                    BfT[:], ps(bank * 512, 256, 0, 64), bBR[:]),
                   rd=[ps(bank * 512, 256), bBR], wr=[BfT])
            else:
                for k in range(8):
                    pe(lambda e, k=k, bank=bank: e.matmul(
                        ps(bank * 512, 256, 0, 64), wbcR[:, k, 64:128], uT[:, k, :],
                        start=(k == 0), stop=(k == 7)),
                       rd=[wbcR, uT], wr=[ps(bank * 512, 256)])
                ve(lambda e, bank=bank: e.tensor_scalar_add(
                    CfT[:], ps(bank * 512, 256, 0, 64), bCR[:]),
                   rd=[ps(bank * 512, 256), bCR], wr=[CfT])

        xs_dt_chunk(j, 0)
        if j + 1 < n_tiles:
            ln_stats(lambda c, xnv=xnv: xnv[:, c, :])
            ln_rstd(RSTD1 + 2 * npar, NEGM1 + 2 * npar)
            for c in range(NCH):
                dg_build(npar * 2 + c, RSTD1 + 2 * npar + c)
                negmr_row(NEGM1 + 2 * npar + c, R1ROW + (npar * 2 + c) * 128)
        ssd_chunk(0)
        outproj_chunk(j, 0, xv)

        xs_dt_chunk(j, 1)
        ssd_chunk(1)
        outproj_chunk(j, 1, xv)
        ln_stats(lambda c: rT[:, c, :])
        ln_rstd(RSTD2, NEGM2)
        for c in range(NCH):
            dg_build(4 + c, RSTD2 + c)
            negmr_row(NEGM2 + c, R2ROW + c * 128)
        for c in range(NCH):
            ve(lambda e, c=c: e.tensor_scalar(
                x1tmT[:, c, :], rT[:, c, :],
                statT[:, RSTD2 + c:RSTD2 + c + 1],
                statT[:, NEGM2 + c:NEGM2 + c + 1], ALU.mult, ALU.add),
               rd=[rT, statT], wr=[x1tmT])
        for c in range(NCH):
            for f in range(8):
                pe(lambda e, c=c, f=f: e.matmul(
                    ps(c * 1024 + f * 128, 128), rT[:, c, f * 128:(f + 1) * 128],
                    DgT[:, (4 + c) * 128:(5 + c) * 128], start=True, stop=False),
                   rd=[rT, DgT], wr=[ps(c * 1024 + f * 128, 128)])
                pe(lambda e, c=c, f=f: e.matmul(
                    ps(c * 1024 + f * 128, 128), onesR[0:1, 0:128],
                    row_ap(R2ROW + c * 128), start=False, stop=True),
                   rd=[onesR, rowT], wr=[ps(c * 1024 + f * 128, 128)])
            ve(lambda e, c=c: e.tensor_copy(
                uT[:, :, c * 128:(c + 1) * 128],
                ps(c * 1024, 1024).rearrange("p (a b) -> p a b", a=8)),
               rd=[ps(c * 1024, 1024)], wr=[uT])
        ve(lambda e: e.tensor_copy(u16[:, 0].rearrange("p a b -> p (a b)"),
                                   uT[:].rearrange("p a b -> p (a b)")),
           rd=[uT], wr=[u16])
        ve(lambda e: e.tensor_sub(u16[:, 1].rearrange("p a b -> p (a b)"),
                                  uT[:].rearrange("p a b -> p (a b)"),
                                  u16[:, 0].rearrange("p a b -> p (a b)")),
           rd=[uT, u16], wr=[u16])
        if j + 2 < n_tiles:
            dma("sync", xv[:], x_hbm(j + 2), dX[par])

        for k in range(8):
            buf = k % 2
            for mt in range(2):
                for pi, (wh, uh) in enumerate(PASSES):
                    st, sp = (k == 0 and pi == 0), (k == 7 and pi == 2)
                    pe(lambda e, mt=mt, k=k, buf=buf, wh=wh, uh=uh, st=st,
                       sp=sp: e.matmul(
                        ps(mt * 512, 256),
                        u1B[:, buf, wh, mt * 128:(mt + 1) * 128],
                        u16[:, uh, k, :], start=st, stop=sp),
                       rd=[u1B[:, buf], u16], wr=[ps(mt * 512, 256)])
            nxt = 8 * j + k + 3
            if nxt <= 8 * n_tiles:
                dma("gpsimd", u1B[:, buf], u1HL_hbm((nxt - 1) % 8),
                    dU1[(nxt - 1) % 2])
        for mt in range(2):
            ve(lambda e, mt=mt: e.tensor_copy(P16[:, 0, mt, :],
                                              ps(mt * 512, 256)),
               rd=[ps(mt * 512, 256)], wr=[P16])
            ve(lambda e, mt=mt: e.tensor_sub(P16[:, 1, mt, :],
                                             ps(mt * 512, 256),
                                             P16[:, 0, mt, :]),
               rd=[ps(mt * 512, 256), P16], wr=[P16])
        for m in range(32):
            buf = m % 2
            hoff = 1024 + (m % 2) * 512
            for k2 in range(2):
                for pi, (wh, ph) in enumerate(PASSES):
                    st, sp = (k2 == 0 and pi == 0), (k2 == 1 and pi == 2)
                    pe(lambda e, buf=buf, hoff=hoff, k2=k2, wh=wh, ph=ph,
                       st=st, sp=sp:
                       e.matmul(ps(hoff, 256), v1B[:, buf, wh, k2, :],
                                P16[:, ph, k2, :], start=st, stop=sp),
                       rd=[v1B[:, buf], P16], wr=[ps(hoff, 256)])
            sc(lambda e, m=m, hoff=hoff: e.activation(
                HsT[m % 2][:], ps(hoff, 256), AF.Gelu_apprx_tanh,
                bias=b1R[:, m:m + 1]),
               rd=[ps(hoff, 256), b1R], wr=[HsT[m % 2]])
            ve(lambda e, m=m: e.tensor_copy(H16[m % 2][:, 0, :], HsT[m % 2][:]),
               rd=[HsT[m % 2]], wr=[H16[m % 2]])
            ve(lambda e, m=m: e.tensor_sub(H16[m % 2][:, 1, :], HsT[m % 2][:],
                                           H16[m % 2][:, 0, :]),
               rd=[HsT[m % 2], H16[m % 2]], wr=[H16[m % 2]])
            for mt in range(2):
                for pi, (wh, hh) in enumerate(PASSES):
                    st, sp = (m == 0 and pi == 0), (m == 31 and pi == 2)
                    pe(lambda e, m=m, mt=mt, buf=buf, wh=wh, hh=hh, st=st,
                       sp=sp: e.matmul(
                        ps(2048 + mt * 512, 256),
                        u2B[:, buf, wh, mt * 128:(mt + 1) * 128],
                        H16[m % 2][:, hh, :], start=st, stop=sp),
                       rd=[u2B[:, buf], H16[m % 2]],
                       wr=[ps(2048 + mt * 512, 256)])
            if m + 2 < 32:
                dma("gpsimd", v1B[:, buf], v1HL_hbm(m + 2), dV1[m % 2])
                dma("gpsimd", u2B[:, buf], u2HL_hbm(m + 2), dU2[m % 2])
            elif j + 1 < n_tiles:
                dma("gpsimd", v1B[:, buf], v1HL_hbm(m - 30), dV1[m % 2])
                dma("gpsimd", u2B[:, buf], u2HL_hbm(m - 30), dU2[m % 2])
        ve(lambda e: e.tensor_copy(QT[:, 0, :], ps(2048, 256)),
           rd=[ps(2048, 256)], wr=[QT])
        ve(lambda e: e.tensor_copy(QT[:, 1, :], ps(2560, 256)),
           rd=[ps(2560, 256)], wr=[QT])
        for c in range(NCH):
            for k2 in range(2):
                for n2 in range(2):
                    pe(lambda e, c=c, k2=k2, n2=n2: e.matmul(
                        ps(3072 + n2 * 512, 512), QT[:, k2, c * 128:(c + 1) * 128],
                        v2R[:, k2, n2 * 512:(n2 + 1) * 512],
                        start=(k2 == 0), stop=False),
                       rd=[QT, v2R], wr=[ps(3072 + n2 * 512, 512)])
            for n2 in range(2):
                pe(lambda e, n2=n2: e.matmul(
                    ps(3072 + n2 * 512, 512), onesR[0:1, 0:128],
                    b2R[0:1, n2 * 512:(n2 + 1) * 512], start=False, stop=True),
                   rd=[onesR, b2R], wr=[ps(3072 + n2 * 512, 512)])
            ve(lambda e, c=c: e.tensor_add(
                x1tmT[:, c, :], ps(3072, 1024), x1tmT[:, c, :]),
               rd=[ps(3072, 1024), x1tmT], wr=[x1tmT])
        ln_stats(lambda c: x1tmT[:, c, :])
        ln_rstd(RSTD3, NEGM3)
        for c in range(NCH):
            negmr_row(NEGM3 + c, R3ROW + c * 128)
        for c in range(NCH):
            ve(lambda e, c=c: e.tensor_scalar(
                rT[:, c, :], x1tmT[:, c, :],
                statT[:, RSTD3 + c:RSTD3 + c + 1],
                statT[:, NEGM3 + c:NEGM3 + c + 1], ALU.mult, ALU.add),
               rd=[x1tmT, statT], wr=[rT])
        for c in range(NCH):
            dma("sync", AP(dout, j * TILE * D + c * CH * D,
                           [[D, 128], [1, D]]), rT[:, c, :], dOUT)
        if j + 1 < n_tiles:
            dma("sync", wzB[:, 0], wzHL_hbm(0), dWZ[0])
            dma("sync", wzB[:, 1], wzHL_hbm(1), dWZ[1])

    fin = dOUT.n

    def final_wait(e, fin=fin):
        e.wait_ge(dOUT.sem, fin)
        return None
    cur["sync"].append(final_wait)

    for bi, blk in enumerate(blocks):
        with nc.Block(f"blk{bi}", no_gpsimd_drain=True) as B:
            for eng in ENGINES:
                fns = blk[eng]
                if not fns:
                    continue
                getattr(B, eng)(lambda E, fns=fns: [f(E) for f in fns])
    ctx.close()
    return nc


def _prep_shared(L0_attn_W_in, L0_attn_b_in, L0_attn_A_log, L0_attn_D_skip,
                 L0_attn_dt_bias, L0_attn_W_out, L0_attn_b_out, L0_attn_ln_gamma,
                 L0_attn_ln_beta, L0_ffn_U1, L0_ffn_V1, L0_ffn_b1, L0_ffn_U2,
                 L0_ffn_V2, L0_ffn_b2):
    W = np.asarray(L0_attn_W_in, np.float32)
    b = np.asarray(L0_attn_b_in, np.float32)
    c = np.ascontiguousarray
    m = {}
    m["wz"] = c(W[:, 0:2048].reshape(8, 128, 16, 128).transpose(2, 1, 0, 3))
    m["wx"] = c(W[:, 2048:4096].reshape(8, 128, 2048))
    m["wbc"] = c(W[:, 4096:4224].reshape(8, 128, 128).transpose(1, 0, 2))
    m["wdt"] = c(W[:, 4224:4232].reshape(8, 128, 8).transpose(1, 0, 2))
    m["wo"] = c(np.asarray(L0_attn_W_out, np.float32).reshape(16, 128, 1024))
    m["u1"] = c(np.asarray(L0_ffn_U1, np.float32).reshape(8, 128, 256))
    m["v1"] = c(np.asarray(L0_ffn_V1, np.float32).reshape(2, 128, 32, 128)
                .transpose(2, 1, 0, 3))
    m["u2"] = c(np.asarray(L0_ffn_U2, np.float32).reshape(32, 128, 256))
    m["v2"] = c(np.asarray(L0_ffn_V2, np.float32).reshape(2, 128, 1024)
                .transpose(1, 0, 2))
    m["bz"] = c(b[0:2048].reshape(16, 128).T)
    m["bx"] = c(b[2048:4096].reshape(1, 2048))
    m["bB"] = c(b[4096:4160].reshape(64, 1))
    m["bC"] = c(b[4160:4224].reshape(64, 1))
    m["bdt"] = c(np.stack([b[4224:4232], np.asarray(L0_attn_dt_bias, np.float32)]))
    m["bo"] = c(np.asarray(L0_attn_b_out, np.float32).reshape(1, 1024))
    m["b1"] = c(np.asarray(L0_ffn_b1, np.float32).reshape(32, 128).T)
    m["b2"] = c(np.asarray(L0_ffn_b2, np.float32).reshape(1, 1024))
    m["gam"] = c(np.asarray(L0_attn_ln_gamma, np.float32).reshape(8, 128).T)
    m["bet"] = c(np.asarray(L0_attn_ln_beta, np.float32).reshape(8, 128).T)
    m["alog"] = c(np.asarray(L0_attn_A_log, np.float32).reshape(1, 8))
    m["dsk"] = c(np.asarray(L0_attn_D_skip, np.float32).reshape(1, 8))
    m["ident"] = np.eye(128, dtype=np.float32)
    m["tri"] = np.triu(np.ones((128, 128), np.float32))
    m["onesc"] = np.ones((128, 128), np.float32)
    return m


def kernel(L0_attn_input_tensor, L0_attn_ln_gamma, L0_attn_ln_beta, L0_attn_W_in,
           L0_attn_b_in, L0_attn_A_log, L0_attn_D_skip, L0_attn_dt_bias,
           L0_attn_W_out, L0_attn_b_out, L0_ffn_U1, L0_ffn_V1, L0_ffn_b1,
           L0_ffn_U2, L0_ffn_V2, L0_ffn_b2):
    x = np.ascontiguousarray(np.asarray(L0_attn_input_tensor, np.float32))
    shared = _prep_shared(L0_attn_W_in, L0_attn_b_in, L0_attn_A_log,
                          L0_attn_D_skip, L0_attn_dt_bias, L0_attn_W_out,
                          L0_attn_b_out, L0_attn_ln_gamma, L0_attn_ln_beta,
                          L0_ffn_U1, L0_ffn_V1, L0_ffn_b1, L0_ffn_U2,
                          L0_ffn_V2, L0_ffn_b2)
    nc = build_nc(SEQ // TILE)
    in_maps = []
    for i in range(8):
        im = dict(shared)
        im["x"] = np.ascontiguousarray(x[i])
        in_maps.append(im)
    res = run_bass_kernel_spmd(nc, in_maps, core_ids=list(range(8)))
    out = np.stack([np.asarray(r["out"], np.float32) for r in res.results])
    return out


# version 06
# speedup vs baseline: 1.1559x; 1.1559x over previous
import numpy as np
from contextlib import ExitStack

import concourse.bass as bass
import concourse.mybir as mybir
from concourse.bass import AP
from concourse.bass_utils import run_bass_kernel_spmd

F32 = mybir.dt.float32
AF = mybir.ActivationFunctionType
ALU = mybir.AluOpType
AX = mybir.AxisListType

D = 1024
SEQ = 4096
HEADS = 8
TILE = 256
CH = 128
NCH = TILE // CH
EPS = 1e-5

ENGINES = ("tensor", "vector", "scalar", "sync", "gpsimd")
DMA_Q = ("sync", "gpsimd")
GRANULE = {"wzB": 1024, "wxB": 2048, "woB": 1024, "v1B": 256, "u2B": 256,
           "u1B": 256, "arow": 1024}


def build_nc(n_tiles=16):
    nc = bass.Bass("TRN2", target_bir_lowering=False)
    ctx = ExitStack()

    dx = nc.dram_tensor("x", [SEQ, D], F32, kind="ExternalInput")
    dout = nc.dram_tensor("out", [SEQ, D], F32, kind="ExternalOutput")
    dwz = nc.dram_tensor("wz", [16, 128, 8, 128], F32, kind="ExternalInput")
    dwx = nc.dram_tensor("wx", [8, 128, 2048], F32, kind="ExternalInput")
    dwbc = nc.dram_tensor("wbc", [128, 8, 128], F32, kind="ExternalInput")
    dwdt = nc.dram_tensor("wdt", [128, 8, 8], F32, kind="ExternalInput")
    dwo = nc.dram_tensor("wo", [16, 128, 1024], F32, kind="ExternalInput")
    du1 = nc.dram_tensor("u1", [8, 128, 256], F32, kind="ExternalInput")
    dv1 = nc.dram_tensor("v1", [32, 128, 2, 128], F32, kind="ExternalInput")
    du2 = nc.dram_tensor("u2", [32, 128, 256], F32, kind="ExternalInput")
    dv2 = nc.dram_tensor("v2", [128, 2, 1024], F32, kind="ExternalInput")
    dbz = nc.dram_tensor("bz", [128, 16], F32, kind="ExternalInput")
    dbx = nc.dram_tensor("bx", [1, 2048], F32, kind="ExternalInput")
    dbB = nc.dram_tensor("bB", [64, 1], F32, kind="ExternalInput")
    dbC = nc.dram_tensor("bC", [64, 1], F32, kind="ExternalInput")
    dbdt = nc.dram_tensor("bdt", [2, 8], F32, kind="ExternalInput")
    dbo = nc.dram_tensor("bo", [1, 1024], F32, kind="ExternalInput")
    db1 = nc.dram_tensor("b1", [128, 32], F32, kind="ExternalInput")
    db2 = nc.dram_tensor("b2", [1, 1024], F32, kind="ExternalInput")
    dgam = nc.dram_tensor("gam", [128, 8], F32, kind="ExternalInput")
    dbet = nc.dram_tensor("bet", [128, 8], F32, kind="ExternalInput")
    dalog = nc.dram_tensor("alog", [1, 8], F32, kind="ExternalInput")
    ddsk = nc.dram_tensor("dsk", [1, 8], F32, kind="ExternalInput")
    dident = nc.dram_tensor("ident", [128, 128], F32, kind="ExternalInput")
    dtri = nc.dram_tensor("tri", [128, 128], F32, kind="ExternalInput")
    dones = nc.dram_tensor("onesc", [128, 128], F32, kind="ExternalInput")

    sb = lambda name, p, f: ctx.enter_context(nc.sbuf_tensor(name, [p, f], F32))
    sb3 = lambda name, p, a, b: ctx.enter_context(nc.sbuf_tensor(name, [p, a, b], F32))
    sb4 = lambda name, p, a, b, c: ctx.enter_context(
        nc.sbuf_tensor(name, [p, a, b, c], F32))

    xT = [sb3("xT0", 128, NCH, 1024), sb3("xT1", 128, NCH, 1024)]
    uT = sb3("uT", 128, 8, TILE)
    szT = sb3("szT", 128, 16, TILE)
    xsT = sb("xsT", 128, 2048)
    BfT = sb("BfT", 64, TILE)
    CfT = sb("CfT", 64, TILE)
    dtrT = sb("dtrT", 128, NCH * 8)
    dtT = sb("dtT", 128, NCH * 8)
    dtfmT = sb("dtfmT", 8, 128)
    ldT = sb("ldT", 8, 128)
    w8T = sb("w8T", 8, 128)
    aT = sb("aT", 8, 128)
    wdt8T = sb("wdt8T", 8, 128)
    expaT = sb("expaT", 8, 128)
    atmT = sb("atmT", 128, 16)
    arow = sb("arow", 1, 2048)
    e8cT = [sb("e8c0", 64, 8), sb("e8c1", 64, 8)]
    BtmT = sb("BtmT", 128, 64)
    BwT = sb("BwT", 128, 512)
    EminT = sb("EminT", 128, 1024)
    GTmT = sb("GTmT", 128, 128)
    MTT = sb("MTT", 128, 1024)
    MTvT = sb("MTvT", 64, 1024)
    hS = sb("hS", 64, 2048)
    gT = sb("gT", 128, 2048)
    rT = sb3("rT", 128, NCH, 1024)
    x1tmT = sb3("x1tm", 128, NCH, 1024)
    r2T = sb3("r2T", 128, NCH, 1024)
    PT = sb3("PT", 128, 2, TILE)
    HsT = [sb("Hs0", 128, TILE), sb("Hs1", 128, TILE)]
    QT = sb3("QT", 128, 2, TILE)
    statT = sb("statT", 128, 48)
    rowT = sb("rowT", 1, 1024)
    DgT = sb("DgT", 128, 768)
    DdT = sb("DdT", 128, 1024)
    DbcT = sb("DbcT", 128, 8)
    AcolT = sb("AcolT", 8, 1)
    ArowT = sb("ArowT", 1, 8)

    wzB = sb4("wzB", 128, 2, 8, 128)
    wxB = sb3("wxB", 128, 2, 2048)
    woB = sb3("woB", 128, 2, 1024)
    v1B = sb4("v1B", 128, 2, 2, 128)
    u2B = sb3("u2B", 128, 2, 256)
    u1B = sb3("u1B", 128, 2, 256)
    v2R = sb3("v2R", 128, 2, 1024)
    wbcR = sb3("wbcR", 128, 8, 128)
    wdtR = sb3("wdtR", 128, 8, 8)
    bzR = sb("bzR", 128, 16)
    bxR = sb("bxR", 1, 2048)
    bBR = sb("bBR", 64, 1)
    bCR = sb("bCR", 64, 1)
    bdtR = sb("bdtR", 2, 8)
    boR = sb("boR", 1, 1024)
    b1R = sb("b1R", 128, 32)
    b2R = sb("b2R", 1, 1024)
    gamR = sb("gamR", 128, 8)
    betR = sb("betR", 128, 8)
    alogR = sb("alogR", 1, 8)
    dskR = sb("dskR", 1, 8)
    identR = sb("identR", 128, 128)
    triR = sb("triR", 128, 128)
    onesR = sb("onesR", 128, 128)

    PS = ctx.enter_context(nc.psum_tensor("PS", [128, 4096], F32))

    def ps(lo, n, p0=0, pn=128):
        return PS[p0:p0 + pn, lo:lo + n]

    class SemC:
        def __init__(self, name):
            self.sem = ctx.enter_context(nc.semaphore(name))
            self.n = 0

    sPE, sV, sS = SemC("sPE"), SemC("sV"), SemC("sS")
    dX = [SemC("dXa"), SemC("dXb")]
    dWZ = [SemC("dWZa"), SemC("dWZb")]
    dWX = [SemC("dWXa"), SemC("dWXb")]
    dWO = [SemC("dWOa"), SemC("dWOb")]
    dU1 = [SemC("dU1a"), SemC("dU1b")]
    dV1 = [SemC("dV1a"), SemC("dV1b")]
    dU2 = [SemC("dU2a"), SemC("dU2b")]
    dRES, dOUT = SemC("dRES"), SemC("dOUT")
    dROW = [SemC("dROWa"), SemC("dROWb")]
    OWN = {"tensor": sPE, "vector": sV, "scalar": sS}

    blocks = []
    cur = {}
    dirty = {"vector": set(), "scalar": set()}
    waited = {}
    lastw = {}
    readers = {}

    def new_block():
        nonlocal cur
        cur = {e: [] for e in ENGINES}
        blocks.append(cur)
        for s in dirty.values():
            s.clear()

    def keys_of(ap):
        t = getattr(ap, "tensor", ap)
        name = getattr(t, "name", None)
        off = getattr(ap, "offset", 0)
        if name == "PS":
            span = 1
            for stp, cnt in list(ap.ap)[1:]:
                span += (cnt - 1) * stp
            return [("PS", b) for b in range(off // 512, (off + span - 1) // 512 + 1)]
        g = GRANULE.get(name)
        if g is not None:
            return [(name, off // g)]
        return [name]

    def step(eng, emit, rd=(), wr=(), extra=(), inc=None, inc_amt=1):
        rdk = [k for a in rd for k in keys_of(a)]
        wrk = [k for a in wr for k in keys_of(a)]
        want = {}

        def need(dep):
            deng, dsem, dval = dep
            if deng == eng and deng not in DMA_Q:
                return
            cur_ = want.get(id(dsem))
            if cur_ is None or cur_[1] < dval:
                want[id(dsem)] = (dsem, dval)

        for k in rdk:
            w = lastw.get(k)
            if w:
                need(w)
        for k in wrk:
            w = lastw.get(k)
            if w:
                need(w)
            for dep in readers.get(k, {}).values():
                need(dep)
        for (s, v) in extra:
            need(("_x", s, v))
        if inc is None:
            inc = OWN.get(eng)
        inc.n += inc_amt
        me = (eng, inc, inc.n)
        for k in rdk:
            readers.setdefault(k, {})[eng] = me
        for k in wrk:
            lastw[k] = me
            readers[k] = {}
        nd = False
        if eng in dirty:
            if (set(rdk) | set(wrk)) & dirty[eng]:
                nd = True
                dirty[eng].clear()
            dirty[eng].update(wrk)
        wl = []
        for (dsem, dval) in want.values():
            key = (eng, id(dsem))
            if waited.get(key, -1) < dval:
                waited[key] = dval
                wl.append((dsem.sem, dval))

        def fn(e, wl=wl, emit=emit, inc=inc, amt=inc_amt, nd=nd):
            if nd:
                e.drain()
            for (sem, v) in wl:
                e.wait_ge(sem, v)
            ins = emit(e)
            if ins is not None:
                ins.then_inc(inc.sem, amt)
            return ins

        cur[eng].append(fn)

    def pe(emit, rd=(), wr=()):
        step("tensor", emit, rd=rd, wr=wr)

    def ve(emit, rd=(), wr=()):
        step("vector", emit, rd=rd, wr=wr)

    def sc(emit, rd=(), wr=()):
        step("scalar", emit, rd=rd, wr=wr)

    def dma(queue, out_ap, in_ap, sem):
        step(queue, lambda e, o=out_ap, i=in_ap: e.dma_start(out=o, in_=i),
             rd=[in_ap], wr=[out_ap], inc=sem, inc_amt=16)

    def setw(buf_ap, sem):
        for k in keys_of(buf_ap):
            lastw[k] = ("sync", sem, sem.n)

    def x_hbm(j):
        return AP(dx, j * TILE * D, [[D, 128], [CH * D, NCH], [1, D]])

    def wz_hbm(ft):
        return AP(dwz, ft * 128 * 8 * 128, [[1024, 128], [128, 8], [1, 128]])

    def wx_hbm(k):
        return AP(dwx, k * 128 * 2048, [[2048, 128], [1, 2048]])

    def wo_hbm(k):
        return AP(dwo, k * 128 * 1024, [[1024, 128], [1, 1024]])

    def v1_hbm(m):
        return AP(dv1, m * 128 * 2 * 128, [[256, 128], [128, 2], [1, 128]])

    def u2_hbm(m):
        return AP(du2, m * 128 * 256, [[256, 128], [1, 256]])

    def u1_hbm(k):
        return AP(du1, k * 128 * 256, [[256, 128], [1, 256]])

    RSTD1, NEGM1 = 0, 4
    SSUM, SSSQ, SVAR, SLNV, SMR = 8, 10, 12, 14, 16
    RSTD2, NEGM2, RSTD3, NEGM3 = 28, 30, 32, 34
    SPE8 = 36
    R1ROW, R2ROW, R3ROW = 0, 512, 768

    def ln_stats(src):
        for c in range(NCH):
            ve(lambda e, c=c, src=src: e.reduce_sum(
                statT[:, SSUM + c:SSUM + c + 1], src(c), axis=AX.X),
               rd=[src(c)], wr=[statT])
            sc(lambda e, c=c, src=src: e.activation(
                EminT[:], src(c), AF.Square,
                accum_out=statT[:, SSSQ + c:SSSQ + c + 1]),
               rd=[src(c)], wr=[EminT, statT])

    def ln_rstd(rcol, ncol):
        ve(lambda e: e.tensor_scalar_mul(
            statT[:, SMR:SMR + 2], statT[:, SSUM:SSUM + 2], 1.0 / D),
           rd=[statT], wr=[statT])
        ve(lambda e: e.tensor_scalar_mul(
            statT[:, SVAR:SVAR + 2], statT[:, SSSQ:SSSQ + 2], 1.0 / D),
           rd=[statT], wr=[statT])
        ve(lambda e: e.tensor_mul(statT[:, SLNV:SLNV + 2], statT[:, SMR:SMR + 2],
                                  statT[:, SMR:SMR + 2]), rd=[statT], wr=[statT])
        ve(lambda e: e.tensor_sub(statT[:, SVAR:SVAR + 2], statT[:, SVAR:SVAR + 2],
                                  statT[:, SLNV:SLNV + 2]), rd=[statT], wr=[statT])
        sc(lambda e: e.activation(statT[:, SLNV:SLNV + 2], statT[:, SVAR:SVAR + 2],
                                  AF.Ln, bias=statT[:, 47:48]),
           rd=[statT], wr=[statT])
        sc(lambda e, rcol=rcol: e.activation(
            statT[:, rcol:rcol + 2], statT[:, SLNV:SLNV + 2], AF.Exp, scale=-0.5),
           rd=[statT], wr=[statT])
        ve(lambda e, rcol=rcol: e.tensor_mul(
            statT[:, SMR:SMR + 2], statT[:, SMR:SMR + 2], statT[:, rcol:rcol + 2]),
           rd=[statT], wr=[statT])
        ve(lambda e, ncol=ncol: e.tensor_scalar_mul(
            statT[:, ncol:ncol + 2], statT[:, SMR:SMR + 2], -1.0),
           rd=[statT], wr=[statT])

    def negmr_row(col, row_off):
        pe(lambda e, col=col: e.transpose(ps(3968, 128, 0, 1),
                                          statT[:, col:col + 1], identR[:]),
           rd=[statT, identR], wr=[ps(3968, 128)])
        ve(lambda e, ro=row_off: e.tensor_copy(
            AP(rowT, ro, [[1024, 1], [1, 128]]), ps(3968, 128, 0, 1)),
           rd=[ps(3968, 128)], wr=[rowT])

    def dg_build(slot, col):
        ve(lambda e, slot=slot, col=col: e.tensor_scalar_mul(
            DgT[:, slot * 128:(slot + 1) * 128], identR[:], statT[:, col:col + 1]),
           rd=[identR, statT], wr=[DgT])

    def row_ap(off):
        return AP(rowT, off, [[1024, 1], [1, 128]])

    new_block()
    res_list = [
        (v2R[:], dv2.ap()), (wbcR[:], dwbc.ap()),
        (wdtR[:], dwdt.ap()), (bzR[:], dbz.ap()), (bxR[:], dbx.ap()),
        (bBR[:], dbB.ap()), (bCR[:], dbC.ap()), (bdtR[:], dbdt.ap()),
        (boR[:], dbo.ap()), (b1R[:], db1.ap()), (b2R[:], db2.ap()),
        (gamR[:], dgam.ap()), (betR[:], dbet.ap()), (alogR[:], dalog.ap()),
        (dskR[:], ddsk.ap()), (identR[:], dident.ap()), (triR[:], dtri.ap()),
        (onesR[:], dones.ap()),
    ]
    for (dst, src) in res_list:
        dma("sync", dst, src, dRES)
    for (dst, src) in res_list:
        setw(dst, dRES)
    dma("sync", xT[0][:], x_hbm(0), dX[0])
    if n_tiles > 1:
        dma("sync", xT[1][:], x_hbm(1), dX[1])
    dma("sync", wzB[:, 0], wz_hbm(0), dWZ[0])
    dma("sync", wzB[:, 1], wz_hbm(1), dWZ[1])
    dma("gpsimd", wxB[:, 0], wx_hbm(0), dWX[0])
    dma("gpsimd", wxB[:, 1], wx_hbm(1), dWX[1])
    dma("sync", woB[:, 0], wo_hbm(0), dWO[0])
    dma("sync", woB[:, 1], wo_hbm(1), dWO[1])
    dma("gpsimd", u1B[:, 0], u1_hbm(0), dU1[0])
    dma("gpsimd", u1B[:, 1], u1_hbm(1), dU1[1])
    dma("gpsimd", v1B[:, 0], v1_hbm(0), dV1[0])
    dma("gpsimd", v1B[:, 1], v1_hbm(1), dV1[1])
    dma("gpsimd", u2B[:, 0], u2_hbm(0), dU2[0])
    dma("gpsimd", u2B[:, 1], u2_hbm(1), dU2[1])

    sc(lambda e: e.activation(ArowT[:], alogR[:], AF.Exp),
       rd=[alogR], wr=[ArowT])
    ve(lambda e: e.tensor_scalar_mul(ArowT[:], ArowT[:], -1.0),
       rd=[ArowT], wr=[ArowT])
    pe(lambda e: e.transpose(ps(0, 1, 0, 8), ArowT[:], identR[0:1, 0:1]),
       rd=[ArowT, identR], wr=[ps(0, 1)])
    ve(lambda e: e.tensor_copy(AcolT[:], ps(0, 1, 0, 8)),
       rd=[ps(0, 1)], wr=[AcolT])
    pe(lambda e: e.matmul(ps(512, 8), onesR[0:1, 0:128], dskR[:],
                          start=True, stop=True),
       rd=[onesR, dskR], wr=[ps(512, 8)])
    ve(lambda e: e.tensor_copy(DbcT[:], ps(512, 8)), rd=[ps(512, 8)], wr=[DbcT])
    for h in range(HEADS):
        ve(lambda e, h=h: e.tensor_scalar_mul(DdT[:, h * 128:(h + 1) * 128],
                                              identR[:], DbcT[:, h:h + 1]),
           rd=[identR, DbcT], wr=[DdT])
    ve(lambda e: e.memset(hS[:], 0.0), wr=[hS])
    ve(lambda e: e.memset(statT[:, 47:48], EPS), wr=[statT])

    ln_stats(lambda c: xT[0][:, c, :])
    ln_rstd(RSTD1, NEGM1)
    for c in range(NCH):
        dg_build(c, RSTD1 + c)
        negmr_row(NEGM1 + c, R1ROW + c * 128)


    def chain_sv(c):
        sc(lambda e, c=c: e.activation(statT[:, SPE8:SPE8 + 8],
                                       dtrT[:, c * 8:(c + 1) * 8], AF.Exp),
           rd=[dtrT], wr=[statT])
        sc(lambda e, c=c: e.activation(dtT[:, c * 8:(c + 1) * 8],
                                       statT[:, SPE8:SPE8 + 8], AF.Ln,
                                       bias=onesR[:, 0:1]),
           rd=[statT, onesR], wr=[dtT])

    def chain_pe_a(c):
        pe(lambda e, c=c: e.transpose(ps(1152, 128, 0, 8),
                                      dtT[:, c * 8:(c + 1) * 8], identR[:]),
           rd=[dtT, identR], wr=[ps(1152, 128)])
        ve(lambda e: e.tensor_copy(dtfmT[:], ps(1152, 128, 0, 8)),
           rd=[ps(1152, 128)], wr=[dtfmT])
        ve(lambda e: e.tensor_scalar_mul(ldT[:], dtfmT[:], AcolT[:]),
           rd=[dtfmT, AcolT], wr=[ldT])
        ve(lambda e: e.tensor_tensor_scan(aT[:], ldT[:], onesR[0:8, 0:128],
                                          0.0, ALU.add, ALU.mult),
           rd=[ldT, onesR], wr=[aT])
        sc(lambda e: e.activation(expaT[:], aT[:], AF.Exp), rd=[aT], wr=[expaT])
        sc(lambda e: e.activation(w8T[:], aT[:], AF.Exp, scale=-1.0,
                                  bias=aT[:, 127:128]), rd=[aT], wr=[w8T])
        ve(lambda e: e.tensor_mul(wdt8T[:], w8T[:], dtfmT[:]),
           rd=[w8T, dtfmT], wr=[wdt8T])
        pe(lambda e: e.transpose(ps(1280, 8), aT[:], identR[0:8, 0:8]),
           rd=[aT, identR], wr=[ps(1280, 8)])
        pe(lambda e: e.transpose(ps(1288, 8), wdt8T[:], identR[0:8, 0:8]),
           rd=[wdt8T, identR], wr=[ps(1288, 8)])
        ve(lambda e: e.tensor_copy(atmT[:], ps(1280, 16)),
           rd=[ps(1280, 16)], wr=[atmT])
        dma("sync", AP(arow, 0, [[2048, 1], [128, 8], [1, 128]]), aT[:], dROW[0])
        dma("sync", AP(arow, 1024, [[2048, 1], [128, 8], [1, 128]]), expaT[:],
            dROW[1])

    def chain_pe_b(c):
        for q4 in range(2):
            pe(lambda e, q4=q4: e.matmul(
                ps(q4 * 512, 512), onesR[0:1, 0:128],
                AP(arow, q4 * 512, [[2048, 1], [1, 512]]), start=True, stop=True),
               rd=[onesR, arow], wr=[ps(q4 * 512, 512)])
        pe(lambda e, c=c: e.matmul(ps(1024, 128), BfT[:, c * 128:(c + 1) * 128],
                                   CfT[:, c * 128:(c + 1) * 128],
                                   start=True, stop=True),
           rd=[BfT, CfT], wr=[ps(1024, 128)])
        pe(lambda e, c=c: e.transpose(ps(1296, 64), BfT[:, c * 128:(c + 1) * 128],
                                      identR[0:64, 0:64]),
           rd=[BfT, identR], wr=[ps(1296, 64)])
        pe(lambda e: e.matmul(ps(1408, 8, 0, 64), onesR[0:1, 0:64],
                              AP(arow, 1024 + 127, [[2048, 1], [128, 8]]),
                              start=True, stop=True),
           rd=[onesR, AP(arow, 1024, [[2048, 1], [1, 1024]])], wr=[ps(1408, 8)])

    def chain_v_a(c):
        for h in range(HEADS):
            ve(lambda e, h=h: e.tensor_scalar(
                EminT[:, h * 128:(h + 1) * 128], ps(h * 128, 128),
                atmT[:, h:h + 1], 0.0, ALU.subtract, ALU.min),
               rd=[ps(h * 128, 128), atmT], wr=[EminT])
        ve(lambda e: e.tensor_mul(GTmT[:], ps(1024, 128), triR[:]),
           rd=[ps(1024, 128), triR], wr=[GTmT])
        ve(lambda e: e.tensor_copy(BtmT[:], ps(1296, 64)),
           rd=[ps(1296, 64)], wr=[BtmT])
        ve(lambda e, c=c: e.tensor_copy(e8cT[c][:], ps(1408, 8, 0, 64)),
           rd=[ps(1408, 8)], wr=[e8cT[c]])
        sc(lambda e: e.activation(EminT[:], EminT[:], AF.Exp),
           rd=[EminT], wr=[EminT])

    def chain_v_mt(c):
        for h in range(HEADS):
            ve(lambda e, h=h, c=c: e.scalar_tensor_tensor(
                MTT[:, h * 128:(h + 1) * 128], EminT[:, h * 128:(h + 1) * 128],
                dtT[:, c * 8 + h:c * 8 + h + 1], GTmT[:], ALU.mult, ALU.mult),
               rd=[EminT, dtT, GTmT], wr=[MTT])
        for h in range(HEADS):
            ve(lambda e, h=h: e.tensor_add(MTT[:, h * 128:(h + 1) * 128],
                                           MTT[:, h * 128:(h + 1) * 128],
                                           DdT[:, h * 128:(h + 1) * 128]),
               rd=[MTT, DdT], wr=[MTT])

    def chain_bw(c):
        for h in range(HEADS):
            ve(lambda e, h=h: e.tensor_scalar_mul(
                BwT[:, h * 64:(h + 1) * 64], BtmT[:], atmT[:, 8 + h:9 + h]),
               rd=[BtmT, atmT], wr=[BwT])

    def mtv_pe(c):
        for q4 in range(2):
            pe(lambda e, q4=q4: e.matmul(
                ps(q4 * 512, 512, 0, 64), onesR[0:1, 0:64],
                AP(arow, 1024 + q4 * 512, [[2048, 1], [1, 512]]),
                start=True, stop=True),
               rd=[onesR, AP(arow, 1024, [[2048, 1], [1, 1024]])],
               wr=[ps(q4 * 512, 512)])

    def mtv_v(c):
        for h in range(HEADS):
            ve(lambda e, h=h, c=c: e.tensor_mul(
                MTvT[:, h * 128:(h + 1) * 128], ps(h * 128, 128, 0, 64),
                CfT[:, c * 128:(c + 1) * 128]),
               rd=[ps(h * 128, 128), CfT], wr=[MTvT])

    def y_mms(c):
        for h in range(HEADS):
            for pt in range(2):
                off = 2048 + (2 * h + pt) * 128
                pe(lambda e, h=h, pt=pt, off=off: e.matmul(
                    ps(off, 128),
                    xsT[:, h * 256 + pt * 128:h * 256 + (pt + 1) * 128],
                    MTT[:, h * 128:(h + 1) * 128], start=True, stop=False),
                   rd=[xsT, MTT], wr=[ps(off, 128)])
                pe(lambda e, h=h, pt=pt, off=off: e.matmul(
                    ps(off, 128), hS[:, h * 256 + pt * 128:h * 256 + (pt + 1) * 128],
                    MTvT[:, h * 128:(h + 1) * 128], start=False, stop=True),
                   rd=[hS, MTvT], wr=[ps(off, 128)])

    def gating(c):
        gflat = x1tmT[:].rearrange("p a b -> p (a b)")
        ve(lambda e, c=c, gflat=gflat: e.tensor_tensor(
            gflat.rearrange("p (a b) -> p a b", a=16),
            ps(2048, 2048).rearrange("p (a b) -> p a b", a=16),
            szT[:, :, c * 128:(c + 1) * 128], ALU.mult),
           rd=[ps(2048, 2048), szT], wr=[x1tmT])

    def hinc_mms(c):
        for h in range(HEADS):
            pe(lambda e, h=h: e.matmul(
                ps(2048 + h * 256, 256, 0, 64), BwT[:, h * 64:(h + 1) * 64],
                xsT[:, h * 256:(h + 1) * 256], start=True, stop=True),
               rd=[BwT, xsT], wr=[ps(2048 + h * 256, 256)])

    def hstt(c):
        for h in range(HEADS):
            ve(lambda e, h=h, c=c: e.scalar_tensor_tensor(
                hS[:, h * 256:(h + 1) * 256], hS[:, h * 256:(h + 1) * 256],
                e8cT[c][:, h:h + 1], ps(2048 + h * 256, 256, 0, 64),
                ALU.mult, ALU.add),
               rd=[hS, e8cT[c], ps(2048 + h * 256, 256)], wr=[hS])

    def xs_mms(j, c):
        for k in range(8):
            buf = k % 2
            for n4 in range(4):
                pe(lambda e, k=k, c=c, n4=n4, buf=buf: e.matmul(
                    ps(2048 + n4 * 512, 512), uT[:, k, c * 128:(c + 1) * 128],
                    wxB[:, buf, n4 * 512:(n4 + 1) * 512],
                    start=(k == 0), stop=False),
                   rd=[uT, wxB[:, buf]], wr=[ps(2048 + n4 * 512, 512)])
            nxt = 16 * j + 8 * c + k + 3
            if nxt <= 16 * n_tiles:
                dma("gpsimd", wxB[:, buf], wx_hbm((nxt - 1) % 8),
                    dWX[(nxt - 1) % 2])
        for n4 in range(4):
            pe(lambda e, n4=n4: e.matmul(
                ps(2048 + n4 * 512, 512), onesR[0:1, 0:128],
                bxR[0:1, n4 * 512:(n4 + 1) * 512], start=False, stop=True),
               rd=[onesR, bxR], wr=[ps(2048 + n4 * 512, 512)])
        ve(lambda e: e.tensor_copy(xsT[:], ps(2048, 2048)),
           rd=[ps(2048, 2048)], wr=[xsT])

    def dt_mms(c):
        off = c * 16
        for k in range(8):
            pe(lambda e, c=c, k=k, off=off: e.matmul(
                ps(off, 8), uT[:, k, c * 128:(c + 1) * 128],
                wdtR[:, k, :], start=(k == 0), stop=False),
               rd=[uT, wdtR], wr=[ps(off, 8)])
        pe(lambda e, off=off: e.matmul(ps(off, 8), onesR[0:2, 0:128], bdtR[:],
                                       start=False, stop=True),
           rd=[onesR, bdtR], wr=[ps(off, 8)])
        ve(lambda e, c=c, off=off: e.tensor_copy(dtrT[:, c * 8:(c + 1) * 8],
                                                 ps(off, 8)),
           rd=[ps(off, 8)], wr=[dtrT])

    def outproj_chunk(j, c, xv):
        for k in range(16):
            buf = k % 2
            for n2 in range(2):
                pe(lambda e, k=k, n2=n2, buf=buf: e.matmul(
                    ps(1024 + n2 * 512, 512), gT_slice(k),
                    woB[:, buf, n2 * 512:(n2 + 1) * 512],
                    start=(k == 0), stop=False),
                   rd=[x1tmT, woB[:, buf]], wr=[ps(1024 + n2 * 512, 512)])
            nxt = 32 * j + 16 * c + k + 3
            if nxt <= 32 * n_tiles:
                dma("sync", woB[:, buf], wo_hbm((nxt - 1) % 16),
                    dWO[(nxt - 1) % 2])
        for n2 in range(2):
            pe(lambda e, n2=n2: e.matmul(
                ps(1024 + n2 * 512, 512), onesR[0:1, 0:128],
                boR[0:1, n2 * 512:(n2 + 1) * 512], start=False, stop=True),
               rd=[onesR, boR], wr=[ps(1024 + n2 * 512, 512)])
        ve(lambda e, c=c, xv=xv: e.scalar_tensor_tensor(
            rT[:, c, :], xv[:, c, :], 2.0, ps(1024, 1024), ALU.mult, ALU.add),
           rd=[xv, ps(1024, 1024)], wr=[rT])

    def gT_slice(k):
        return AP(x1tmT, k * 128, [[2048, 128], [1, 128]])

    for j in range(n_tiles):
        par = j % 2
        npar = (j + 1) % 2
        xv, xnv = xT[par], xT[npar]

        for c in range(NCH):
            base = c * 1024
            slot = 2 * par + c
            for f in range(8):
                pe(lambda e, c=c, f=f, base=base, slot=slot, xv=xv: e.matmul(
                    ps(base + f * 128, 128), xv[:, c, f * 128:(f + 1) * 128],
                    DgT[:, slot * 128:(slot + 1) * 128], start=True, stop=False),
                   rd=[xv, DgT], wr=[ps(base + f * 128, 128)])
                pe(lambda e, f=f, base=base, slot=slot: e.matmul(
                    ps(base + f * 128, 128), onesR[0:1, 0:128],
                    row_ap(R1ROW + slot * 128), start=False, stop=True),
                   rd=[onesR, rowT], wr=[ps(base + f * 128, 128)])
            for f in range(8):
                ve(lambda e, c=c, f=f, base=base: e.tensor_scalar(
                    uT[:, f, c * 128:(c + 1) * 128], ps(base + f * 128, 128),
                    gamR[:, f:f + 1], betR[:, f:f + 1], ALU.mult, ALU.add),
                   rd=[ps(base + f * 128, 128), gamR, betR], wr=[uT])
        dt_mms(0)
        dt_mms(1)
        chain_sv(0)
        for bc in range(2):
            bank = 4 + bc
            wsl = (0, 64) if bc == 0 else (64, 128)
            for k in range(8):
                pe(lambda e, k=k, bank=bank, wsl=wsl: e.matmul(
                    ps(bank * 512, 256, 0, 64), wbcR[:, k, wsl[0]:wsl[1]],
                    uT[:, k, :], start=(k == 0), stop=(k == 7)),
                   rd=[wbcR, uT], wr=[ps(bank * 512, 256)])
            if bc == 0:
                ve(lambda e, bank=bank: e.tensor_scalar_add(
                    BfT[:], ps(bank * 512, 256, 0, 64), bBR[:]),
                   rd=[ps(bank * 512, 256), bBR], wr=[BfT])
            else:
                ve(lambda e, bank=bank: e.tensor_scalar_add(
                    CfT[:], ps(bank * 512, 256, 0, 64), bCR[:]),
                   rd=[ps(bank * 512, 256), bCR], wr=[CfT])
        for ft in range(16):
            bank = 4 + (ft % 4)
            buf = ft % 2
            for k in range(8):
                pe(lambda e, ft=ft, k=k, buf=buf, bank=bank: e.matmul(
                    ps(bank * 512, 256), wzB[:, buf, k, :], uT[:, k, :],
                    start=(k == 0), stop=(k == 7)),
                   rd=[wzB[:, buf], uT], wr=[ps(bank * 512, 256)])
            sc(lambda e, ft=ft, bank=bank: e.activation(
                szT[:, ft, :], ps(bank * 512, 256), AF.Silu,
                bias=bzR[:, ft:ft + 1]),
               rd=[ps(bank * 512, 256), bzR], wr=[szT])
            if ft + 2 < 16:
                dma("sync", wzB[:, buf], wz_hbm(ft + 2), dWZ[ft % 2])
            if ft == 1:
                chain_pe_a(0)
            elif ft == 3:
                chain_pe_b(0)
                chain_v_a(0)
            elif ft == 5:
                mtv_pe(0)
                mtv_v(0)
                chain_v_mt(0)
                chain_bw(0)
            elif ft == 7:
                chain_sv(1)
                chain_pe_a(1)
            elif ft == 9:
                chain_pe_b(1)
                chain_v_a(1)
            elif ft == 11:
                mtv_pe(1)
        xs_mms(j, 0)
        y_mms(0)
        gating(0)
        hinc_mms(0)
        hstt(0)
        outproj_chunk(j, 0, xv)
        mtv_v(1)
        chain_v_mt(1)
        chain_bw(1)
        xs_mms(j, 1)
        y_mms(1)
        gating(1)
        hinc_mms(1)
        hstt(1)
        outproj_chunk(j, 1, xv)
        if j + 1 < n_tiles:
            ln_stats(lambda c, xnv=xnv: xnv[:, c, :])
            ln_rstd(RSTD1 + 2 * npar, NEGM1 + 2 * npar)
            for c in range(NCH):
                dg_build(npar * 2 + c, RSTD1 + 2 * npar + c)
                negmr_row(NEGM1 + 2 * npar + c, R1ROW + (npar * 2 + c) * 128)
        ln_stats(lambda c: rT[:, c, :])
        ln_rstd(RSTD2, NEGM2)
        for c in range(NCH):
            dg_build(4 + c, RSTD2 + c)
            negmr_row(NEGM2 + c, R2ROW + c * 128)
        for c in range(NCH):
            ve(lambda e, c=c: e.tensor_scalar(
                x1tmT[:, c, :], rT[:, c, :],
                statT[:, RSTD2 + c:RSTD2 + c + 1],
                statT[:, NEGM2 + c:NEGM2 + c + 1], ALU.mult, ALU.add),
               rd=[rT, statT], wr=[x1tmT])
        for c in range(NCH):
            for f in range(8):
                pe(lambda e, c=c, f=f: e.matmul(
                    ps(c * 1024 + f * 128, 128), rT[:, c, f * 128:(f + 1) * 128],
                    DgT[:, (4 + c) * 128:(5 + c) * 128], start=True, stop=False),
                   rd=[rT, DgT], wr=[ps(c * 1024 + f * 128, 128)])
                pe(lambda e, c=c, f=f: e.matmul(
                    ps(c * 1024 + f * 128, 128), onesR[0:1, 0:128],
                    row_ap(R2ROW + c * 128), start=False, stop=True),
                   rd=[onesR, rowT], wr=[ps(c * 1024 + f * 128, 128)])
            ve(lambda e, c=c: e.tensor_copy(
                uT[:, :, c * 128:(c + 1) * 128],
                ps(c * 1024, 1024).rearrange("p (a b) -> p a b", a=8)),
               rd=[ps(c * 1024, 1024)], wr=[uT])
        if j + 2 < n_tiles:
            dma("sync", xv[:], x_hbm(j + 2), dX[par])
        for k in range(8):
            buf = k % 2
            for mt in range(2):
                pe(lambda e, mt=mt, k=k, buf=buf: e.matmul(
                    ps(mt * 512, 256), u1B[:, buf, mt * 128:(mt + 1) * 128],
                    uT[:, k, :], start=(k == 0), stop=(k == 7)),
                   rd=[u1B[:, buf], uT], wr=[ps(mt * 512, 256)])
            nxt = 8 * j + k + 3
            if nxt <= 8 * n_tiles:
                dma("gpsimd", u1B[:, buf], u1_hbm((nxt - 1) % 8),
                    dU1[(nxt - 1) % 2])
        ve(lambda e: e.tensor_copy(PT[:, 0, :], ps(0, 256)),
           rd=[ps(0, 256)], wr=[PT])
        ve(lambda e: e.tensor_copy(PT[:, 1, :], ps(512, 256)),
           rd=[ps(512, 256)], wr=[PT])
        for m in range(32):
            buf = m % 2
            hoff = 1024 + (m % 2) * 512
            pe(lambda e, m=m, buf=buf, hoff=hoff: e.matmul(
                ps(hoff, 256), v1B[:, buf, 0, :], PT[:, 0, :],
                start=True, stop=False),
               rd=[v1B[:, buf], PT], wr=[ps(hoff, 256)])
            pe(lambda e, m=m, buf=buf, hoff=hoff: e.matmul(
                ps(hoff, 256), v1B[:, buf, 1, :], PT[:, 1, :],
                start=False, stop=True),
               rd=[v1B[:, buf], PT], wr=[ps(hoff, 256)])
            sc(lambda e, m=m, hoff=hoff: e.activation(
                HsT[m % 2][:], ps(hoff, 256), AF.Gelu_apprx_tanh,
                bias=b1R[:, m:m + 1]),
               rd=[ps(hoff, 256), b1R], wr=[HsT[m % 2]])
            for mt in range(2):
                pe(lambda e, m=m, mt=mt, buf=buf: e.matmul(
                    ps(2048 + mt * 512, 256), u2B[:, buf, mt * 128:(mt + 1) * 128],
                    HsT[m % 2][:], start=(m == 0), stop=(m == 31)),
                   rd=[u2B[:, buf], HsT[m % 2]], wr=[ps(2048 + mt * 512, 256)])
            if m + 2 < 32:
                dma("gpsimd", v1B[:, buf], v1_hbm(m + 2), dV1[m % 2])
                dma("gpsimd", u2B[:, buf], u2_hbm(m + 2), dU2[m % 2])
            elif j + 1 < n_tiles:
                dma("gpsimd", v1B[:, buf], v1_hbm(m - 30), dV1[m % 2])
                dma("gpsimd", u2B[:, buf], u2_hbm(m - 30), dU2[m % 2])
        ve(lambda e: e.tensor_copy(QT[:, 0, :], ps(2048, 256)),
           rd=[ps(2048, 256)], wr=[QT])
        ve(lambda e: e.tensor_copy(QT[:, 1, :], ps(2560, 256)),
           rd=[ps(2560, 256)], wr=[QT])
        for c in range(NCH):
            for k2 in range(2):
                for n2 in range(2):
                    pe(lambda e, c=c, k2=k2, n2=n2: e.matmul(
                        ps(3072 + n2 * 512, 512), QT[:, k2, c * 128:(c + 1) * 128],
                        v2R[:, k2, n2 * 512:(n2 + 1) * 512],
                        start=(k2 == 0), stop=False),
                       rd=[QT, v2R], wr=[ps(3072 + n2 * 512, 512)])
            for n2 in range(2):
                pe(lambda e, n2=n2: e.matmul(
                    ps(3072 + n2 * 512, 512), onesR[0:1, 0:128],
                    b2R[0:1, n2 * 512:(n2 + 1) * 512], start=False, stop=True),
                   rd=[onesR, b2R], wr=[ps(3072 + n2 * 512, 512)])
            ve(lambda e, c=c: e.tensor_add(
                x1tmT[:, c, :], ps(3072, 1024), x1tmT[:, c, :]),
               rd=[ps(3072, 1024), x1tmT], wr=[x1tmT])
        ln_stats(lambda c: x1tmT[:, c, :])
        ln_rstd(RSTD3, NEGM3)
        for c in range(NCH):
            negmr_row(NEGM3 + c, R3ROW + c * 128)
        for c in range(NCH):
            ve(lambda e, c=c: e.tensor_scalar(
                rT[:, c, :], x1tmT[:, c, :],
                statT[:, RSTD3 + c:RSTD3 + c + 1],
                statT[:, NEGM3 + c:NEGM3 + c + 1], ALU.mult, ALU.add),
               rd=[x1tmT, statT], wr=[rT])
        for c in range(NCH):
            dma("sync", AP(dout, j * TILE * D + c * CH * D,
                           [[D, 128], [1, D]]), rT[:, c, :], dOUT)
        if j + 1 < n_tiles:
            dma("sync", wzB[:, 0], wz_hbm(0), dWZ[0])
            dma("sync", wzB[:, 1], wz_hbm(1), dWZ[1])

    fin = dOUT.n

    def final_wait(e, fin=fin):
        e.wait_ge(dOUT.sem, fin)
        return None
    cur["sync"].append(final_wait)

    for bi, blk in enumerate(blocks):
        with nc.Block(f"blk{bi}", no_gpsimd_drain=True) as B:
            for eng in ENGINES:
                fns = blk[eng]
                if not fns:
                    continue
                getattr(B, eng)(lambda E, fns=fns: [f(E) for f in fns])
    ctx.close()
    return nc


def _prep_shared(L0_attn_W_in, L0_attn_b_in, L0_attn_A_log, L0_attn_D_skip,
                 L0_attn_dt_bias, L0_attn_W_out, L0_attn_b_out, L0_attn_ln_gamma,
                 L0_attn_ln_beta, L0_ffn_U1, L0_ffn_V1, L0_ffn_b1, L0_ffn_U2,
                 L0_ffn_V2, L0_ffn_b2):
    W = np.asarray(L0_attn_W_in, np.float32)
    b = np.asarray(L0_attn_b_in, np.float32)
    c = np.ascontiguousarray
    m = {}
    m["wz"] = c(W[:, 0:2048].reshape(8, 128, 16, 128).transpose(2, 1, 0, 3))
    m["wx"] = c(W[:, 2048:4096].reshape(8, 128, 2048))
    m["wbc"] = c(W[:, 4096:4224].reshape(8, 128, 128).transpose(1, 0, 2))
    m["wdt"] = c(W[:, 4224:4232].reshape(8, 128, 8).transpose(1, 0, 2))
    m["wo"] = c(np.asarray(L0_attn_W_out, np.float32).reshape(16, 128, 1024))
    m["u1"] = c(np.asarray(L0_ffn_U1, np.float32).reshape(8, 128, 256))
    m["v1"] = c(np.asarray(L0_ffn_V1, np.float32).reshape(2, 128, 32, 128)
                .transpose(2, 1, 0, 3))
    m["u2"] = c(np.asarray(L0_ffn_U2, np.float32).reshape(32, 128, 256))
    m["v2"] = c(np.asarray(L0_ffn_V2, np.float32).reshape(2, 128, 1024)
                .transpose(1, 0, 2))
    m["bz"] = c(b[0:2048].reshape(16, 128).T)
    m["bx"] = c(b[2048:4096].reshape(1, 2048))
    m["bB"] = c(b[4096:4160].reshape(64, 1))
    m["bC"] = c(b[4160:4224].reshape(64, 1))
    m["bdt"] = c(np.stack([b[4224:4232], np.asarray(L0_attn_dt_bias, np.float32)]))
    m["bo"] = c(np.asarray(L0_attn_b_out, np.float32).reshape(1, 1024))
    m["b1"] = c(np.asarray(L0_ffn_b1, np.float32).reshape(32, 128).T)
    m["b2"] = c(np.asarray(L0_ffn_b2, np.float32).reshape(1, 1024))
    m["gam"] = c(np.asarray(L0_attn_ln_gamma, np.float32).reshape(8, 128).T)
    m["bet"] = c(np.asarray(L0_attn_ln_beta, np.float32).reshape(8, 128).T)
    m["alog"] = c(np.asarray(L0_attn_A_log, np.float32).reshape(1, 8))
    m["dsk"] = c(np.asarray(L0_attn_D_skip, np.float32).reshape(1, 8))
    m["ident"] = np.eye(128, dtype=np.float32)
    m["tri"] = np.triu(np.ones((128, 128), np.float32))
    m["onesc"] = np.ones((128, 128), np.float32)
    return m


def kernel(L0_attn_input_tensor, L0_attn_ln_gamma, L0_attn_ln_beta, L0_attn_W_in,
           L0_attn_b_in, L0_attn_A_log, L0_attn_D_skip, L0_attn_dt_bias,
           L0_attn_W_out, L0_attn_b_out, L0_ffn_U1, L0_ffn_V1, L0_ffn_b1,
           L0_ffn_U2, L0_ffn_V2, L0_ffn_b2):
    x = np.ascontiguousarray(np.asarray(L0_attn_input_tensor, np.float32))
    shared = _prep_shared(L0_attn_W_in, L0_attn_b_in, L0_attn_A_log,
                          L0_attn_D_skip, L0_attn_dt_bias, L0_attn_W_out,
                          L0_attn_b_out, L0_attn_ln_gamma, L0_attn_ln_beta,
                          L0_ffn_U1, L0_ffn_V1, L0_ffn_b1, L0_ffn_U2,
                          L0_ffn_V2, L0_ffn_b2)
    nc = build_nc(SEQ // TILE)
    in_maps = []
    for i in range(8):
        im = dict(shared)
        im["x"] = np.ascontiguousarray(x[i])
        in_maps.append(im)
    res = run_bass_kernel_spmd(nc, in_maps, core_ids=list(range(8)))
    out = np.stack([np.asarray(r["out"], np.float32) for r in res.results])
    return out


# version 07
# speedup vs baseline: 1.2051x; 1.0426x over previous
import numpy as np
from contextlib import ExitStack

import concourse.bass as bass
import concourse.mybir as mybir
from concourse.bass import AP
from concourse.bass_utils import run_bass_kernel_spmd

F32 = mybir.dt.float32
AF = mybir.ActivationFunctionType
ALU = mybir.AluOpType
AX = mybir.AxisListType

D = 1024
SEQ = 4096
HEADS = 8
TILE = 256
CH = 128
NCH = TILE // CH
EPS = 1e-5

ENGINES = ("tensor", "vector", "scalar", "sync", "gpsimd")
DMA_Q = ("sync", "gpsimd")
GRANULE = {"wzB": 1024, "wxB": 2048, "woB": 1024, "v1B": 256, "u2B": 256,
           "u1B": 256, "arow": 1024}


def build_nc(n_tiles=16):
    nc = bass.Bass("TRN2", target_bir_lowering=False)
    ctx = ExitStack()

    dx = nc.dram_tensor("x", [SEQ, D], F32, kind="ExternalInput")
    dout = nc.dram_tensor("out", [SEQ, D], F32, kind="ExternalOutput")
    dwz = nc.dram_tensor("wz", [16, 128, 8, 128], F32, kind="ExternalInput")
    dwx = nc.dram_tensor("wx", [8, 128, 2048], F32, kind="ExternalInput")
    dwbc = nc.dram_tensor("wbc", [128, 8, 128], F32, kind="ExternalInput")
    dwdt = nc.dram_tensor("wdt", [128, 8, 8], F32, kind="ExternalInput")
    dwo = nc.dram_tensor("wo", [16, 128, 1024], F32, kind="ExternalInput")
    du1 = nc.dram_tensor("u1", [8, 128, 256], F32, kind="ExternalInput")
    dv1 = nc.dram_tensor("v1", [32, 128, 2, 128], F32, kind="ExternalInput")
    du2 = nc.dram_tensor("u2", [32, 128, 256], F32, kind="ExternalInput")
    dv2 = nc.dram_tensor("v2", [128, 2, 1024], F32, kind="ExternalInput")
    dbz = nc.dram_tensor("bz", [128, 16], F32, kind="ExternalInput")
    dbx = nc.dram_tensor("bx", [1, 2048], F32, kind="ExternalInput")
    dbB = nc.dram_tensor("bB", [64, 1], F32, kind="ExternalInput")
    dbC = nc.dram_tensor("bC", [64, 1], F32, kind="ExternalInput")
    dbdt = nc.dram_tensor("bdt", [2, 8], F32, kind="ExternalInput")
    dbo = nc.dram_tensor("bo", [1, 1024], F32, kind="ExternalInput")
    db1 = nc.dram_tensor("b1", [128, 32], F32, kind="ExternalInput")
    db2 = nc.dram_tensor("b2", [1, 1024], F32, kind="ExternalInput")
    dgam = nc.dram_tensor("gam", [128, 8], F32, kind="ExternalInput")
    dbet = nc.dram_tensor("bet", [128, 8], F32, kind="ExternalInput")
    dalog = nc.dram_tensor("alog", [1, 8], F32, kind="ExternalInput")
    ddsk = nc.dram_tensor("dsk", [1, 8], F32, kind="ExternalInput")
    dident = nc.dram_tensor("ident", [128, 128], F32, kind="ExternalInput")
    dtri = nc.dram_tensor("tri", [128, 128], F32, kind="ExternalInput")
    dones = nc.dram_tensor("onesc", [128, 128], F32, kind="ExternalInput")

    sb = lambda name, p, f: ctx.enter_context(nc.sbuf_tensor(name, [p, f], F32))
    sb3 = lambda name, p, a, b: ctx.enter_context(nc.sbuf_tensor(name, [p, a, b], F32))
    sb4 = lambda name, p, a, b, c: ctx.enter_context(
        nc.sbuf_tensor(name, [p, a, b, c], F32))

    xT = [sb3("xT0", 128, NCH, 1024), sb3("xT1", 128, NCH, 1024)]
    uT = sb3("uT", 128, 8, TILE)
    szT = sb3("szT", 128, 16, TILE)
    xsT = sb("xsT", 128, 2048)
    BfT = sb("BfT", 64, TILE)
    CfT = sb("CfT", 64, TILE)
    dtrT = sb("dtrT", 128, NCH * 8)
    dtT = sb("dtT", 128, NCH * 8)
    dtfmT = sb("dtfmT", 8, 128)
    ldT = sb("ldT", 8, 128)
    w8T = sb("w8T", 8, 128)
    aT = sb("aT", 8, 128)
    wdt8T = sb("wdt8T", 8, 128)
    expaT = sb("expaT", 8, 128)
    atmT = sb("atmT", 128, 16)
    arow = sb("arow", 1, 2048)
    e8cT = [sb("e8c0", 64, 8), sb("e8c1", 64, 8)]
    BtmT = sb("BtmT", 128, 64)
    BwT = sb("BwT", 128, 512)
    EminT = sb("EminT", 128, 1024)
    GTmT = sb("GTmT", 128, 128)
    MTT = sb("MTT", 128, 1024)
    MTvT = sb("MTvT", 64, 1024)
    hS = sb("hS", 64, 2048)
    rT = sb3("rT", 128, NCH, 1024)
    x1tmT = sb3("x1tm", 128, NCH, 1024)
    PT = sb3("PT", 128, 2, TILE)
    HsT = [sb("Hs0", 128, TILE), sb("Hs1", 128, TILE)]
    QT = sb3("QT", 128, 2, TILE)
    statT = sb("statT", 128, 48)
    rowT = sb("rowT", 1, 1024)
    DgT = sb("DgT", 128, 768)
    DdT = sb("DdT", 128, 1024)
    DbcT = sb("DbcT", 128, 8)
    AcolT = sb("AcolT", 8, 1)
    ArowT = sb("ArowT", 1, 8)

    wzB = sb4("wzB", 128, 3, 8, 128)
    wxB = sb3("wxB", 128, 3, 2048)
    woB = sb3("woB", 128, 3, 1024)
    v1B = sb4("v1B", 128, 3, 2, 128)
    u2B = sb3("u2B", 128, 3, 256)
    u1B = sb3("u1B", 128, 2, 256)
    v2R = sb3("v2R", 128, 2, 1024)
    wbcR = sb3("wbcR", 128, 8, 128)
    wdtR = sb3("wdtR", 128, 8, 8)
    bzR = sb("bzR", 128, 16)
    bxR = sb("bxR", 1, 2048)
    bBR = sb("bBR", 64, 1)
    bCR = sb("bCR", 64, 1)
    bdtR = sb("bdtR", 2, 8)
    boR = sb("boR", 1, 1024)
    b1R = sb("b1R", 128, 32)
    b2R = sb("b2R", 1, 1024)
    gamR = sb("gamR", 128, 8)
    betR = sb("betR", 128, 8)
    alogR = sb("alogR", 1, 8)
    dskR = sb("dskR", 1, 8)
    identR = sb("identR", 128, 128)
    triR = sb("triR", 128, 128)
    onesR = sb("onesR", 128, 128)

    PS = ctx.enter_context(nc.psum_tensor("PS", [128, 4096], F32))

    def ps(lo, n, p0=0, pn=128):
        return PS[p0:p0 + pn, lo:lo + n]

    class SemC:
        def __init__(self, name):
            self.sem = ctx.enter_context(nc.semaphore(name))
            self.n = 0

    sPE, sV, sS = SemC("sPE"), SemC("sV"), SemC("sS")
    dX = [SemC("dXa"), SemC("dXb")]
    dWZ = [SemC("dWZ%d" % i) for i in range(3)]
    dWX = [SemC("dWX%d" % i) for i in range(3)]
    dWO = [SemC("dWO%d" % i) for i in range(3)]
    dU1 = [SemC("dU1a"), SemC("dU1b")]
    dV1 = [SemC("dV1%d" % i) for i in range(3)]
    dU2 = [SemC("dU2%d" % i) for i in range(3)]
    dRES, dOUT = SemC("dRES"), SemC("dOUT")
    dROW = [SemC("dROWa"), SemC("dROWb")]
    OWN = {"tensor": sPE, "vector": sV, "scalar": sS}

    blocks = []
    cur = {}
    dirty = {"vector": set(), "scalar": set()}
    waited = {}
    lastw = {}
    readers = {}

    def new_block():
        nonlocal cur
        cur = {e: [] for e in ENGINES}
        blocks.append(cur)
        for s in dirty.values():
            s.clear()

    def keys_of(ap):
        t = getattr(ap, "tensor", ap)
        name = getattr(t, "name", None)
        off = getattr(ap, "offset", 0)
        if name == "PS":
            span = 1
            for stp, cnt in list(ap.ap)[1:]:
                span += (cnt - 1) * stp
            return [("PS", b) for b in range(off // 512, (off + span - 1) // 512 + 1)]
        g = GRANULE.get(name)
        if g is not None:
            return [(name, off // g)]
        return [name]

    def step(eng, emit, rd=(), wr=(), extra=(), inc=None, inc_amt=1):
        rdk = [k for a in rd for k in keys_of(a)]
        wrk = [k for a in wr for k in keys_of(a)]
        want = {}

        def need(dep):
            deng, dsem, dval = dep
            if deng == eng and deng not in DMA_Q:
                return
            cur_ = want.get(id(dsem))
            if cur_ is None or cur_[1] < dval:
                want[id(dsem)] = (dsem, dval)

        for k in rdk:
            w = lastw.get(k)
            if w:
                need(w)
        for k in wrk:
            w = lastw.get(k)
            if w:
                need(w)
            for dep in readers.get(k, {}).values():
                need(dep)
        for (s, v) in extra:
            need(("_x", s, v))
        if inc is None:
            inc = OWN.get(eng)
        inc.n += inc_amt
        me = (eng, inc, inc.n)
        for k in rdk:
            readers.setdefault(k, {})[eng] = me
        for k in wrk:
            lastw[k] = me
            readers[k] = {}
        nd = False
        if eng in dirty:
            if (set(rdk) | set(wrk)) & dirty[eng]:
                nd = True
                dirty[eng].clear()
            dirty[eng].update(wrk)
        wl = []
        for (dsem, dval) in want.values():
            key = (eng, id(dsem))
            if waited.get(key, -1) < dval:
                waited[key] = dval
                wl.append((dsem.sem, dval))

        def fn(e, wl=wl, emit=emit, inc=inc, amt=inc_amt, nd=nd):
            if nd:
                e.drain()
            for (sem, v) in wl:
                e.wait_ge(sem, v)
            ins = emit(e)
            if ins is not None:
                ins.then_inc(inc.sem, amt)
            return ins

        cur[eng].append(fn)

    def pe(emit, rd=(), wr=()):
        step("tensor", emit, rd=rd, wr=wr)

    def ve(emit, rd=(), wr=()):
        step("vector", emit, rd=rd, wr=wr)

    def sc(emit, rd=(), wr=()):
        step("scalar", emit, rd=rd, wr=wr)

    def dma(queue, out_ap, in_ap, sem):
        step(queue, lambda e, o=out_ap, i=in_ap: e.dma_start(out=o, in_=i),
             rd=[in_ap], wr=[out_ap], inc=sem, inc_amt=16)

    def setw(buf_ap, sem):
        for k in keys_of(buf_ap):
            lastw[k] = ("sync", sem, sem.n)

    def x_hbm(j):
        return AP(dx, j * TILE * D, [[D, 128], [CH * D, NCH], [1, D]])

    def wz_hbm(ft):
        return AP(dwz, ft * 128 * 8 * 128, [[1024, 128], [128, 8], [1, 128]])

    def wx_hbm(k):
        return AP(dwx, k * 128 * 2048, [[2048, 128], [1, 2048]])

    def wo_hbm(k):
        return AP(dwo, k * 128 * 1024, [[1024, 128], [1, 1024]])

    def v1_hbm(m):
        return AP(dv1, m * 128 * 2 * 128, [[256, 128], [128, 2], [1, 128]])

    def u2_hbm(m):
        return AP(du2, m * 128 * 256, [[256, 128], [1, 256]])

    def u1_hbm(k):
        return AP(du1, k * 128 * 256, [[256, 128], [1, 256]])

    RSTD1, NEGM1 = 0, 4
    SSUM, SSSQ, SVAR, SLNV, SMR = 8, 10, 12, 14, 16
    RSTD2, NEGM2, RSTD3, NEGM3 = 28, 30, 32, 34
    SPE8 = 36
    R1ROW, R2ROW, R3ROW = 0, 512, 768

    def ln_stats(src):
        for c in range(NCH):
            ve(lambda e, c=c, src=src: e.reduce_sum(
                statT[:, SSUM + c:SSUM + c + 1], src(c), axis=AX.X),
               rd=[src(c)], wr=[statT])
            sc(lambda e, c=c, src=src: e.activation(
                EminT[:], src(c), AF.Square,
                accum_out=statT[:, SSSQ + c:SSSQ + c + 1]),
               rd=[src(c)], wr=[EminT, statT])

    def ln_rstd(rcol, ncol):
        ve(lambda e: e.tensor_scalar_mul(
            statT[:, SMR:SMR + 2], statT[:, SSUM:SSUM + 2], 1.0 / D),
           rd=[statT], wr=[statT])
        ve(lambda e: e.tensor_scalar_mul(
            statT[:, SVAR:SVAR + 2], statT[:, SSSQ:SSSQ + 2], 1.0 / D),
           rd=[statT], wr=[statT])
        ve(lambda e: e.tensor_mul(statT[:, SLNV:SLNV + 2], statT[:, SMR:SMR + 2],
                                  statT[:, SMR:SMR + 2]), rd=[statT], wr=[statT])
        ve(lambda e: e.tensor_sub(statT[:, SVAR:SVAR + 2], statT[:, SVAR:SVAR + 2],
                                  statT[:, SLNV:SLNV + 2]), rd=[statT], wr=[statT])
        sc(lambda e: e.activation(statT[:, SLNV:SLNV + 2], statT[:, SVAR:SVAR + 2],
                                  AF.Ln, bias=statT[:, 47:48]),
           rd=[statT], wr=[statT])
        sc(lambda e, rcol=rcol: e.activation(
            statT[:, rcol:rcol + 2], statT[:, SLNV:SLNV + 2], AF.Exp, scale=-0.5),
           rd=[statT], wr=[statT])
        ve(lambda e, rcol=rcol: e.tensor_mul(
            statT[:, SMR:SMR + 2], statT[:, SMR:SMR + 2], statT[:, rcol:rcol + 2]),
           rd=[statT], wr=[statT])
        ve(lambda e, ncol=ncol: e.tensor_scalar_mul(
            statT[:, ncol:ncol + 2], statT[:, SMR:SMR + 2], -1.0),
           rd=[statT], wr=[statT])

    def negmr_row(col, row_off):
        pe(lambda e, col=col: e.transpose(ps(3968, 128, 0, 1),
                                          statT[:, col:col + 1], identR[:]),
           rd=[statT, identR], wr=[ps(3968, 128)])
        ve(lambda e, ro=row_off: e.tensor_copy(
            AP(rowT, ro, [[1024, 1], [1, 128]]), ps(3968, 128, 0, 1)),
           rd=[ps(3968, 128)], wr=[rowT])

    def dg_build(slot, col):
        ve(lambda e, slot=slot, col=col: e.tensor_scalar_mul(
            DgT[:, slot * 128:(slot + 1) * 128], identR[:], statT[:, col:col + 1]),
           rd=[identR, statT], wr=[DgT])

    def row_ap(off):
        return AP(rowT, off, [[1024, 1], [1, 128]])

    new_block()
    res_list = [
        (v2R[:], dv2.ap()), (wbcR[:], dwbc.ap()),
        (wdtR[:], dwdt.ap()), (bzR[:], dbz.ap()), (bxR[:], dbx.ap()),
        (bBR[:], dbB.ap()), (bCR[:], dbC.ap()), (bdtR[:], dbdt.ap()),
        (boR[:], dbo.ap()), (b1R[:], db1.ap()), (b2R[:], db2.ap()),
        (gamR[:], dgam.ap()), (betR[:], dbet.ap()), (alogR[:], dalog.ap()),
        (dskR[:], ddsk.ap()), (identR[:], dident.ap()), (triR[:], dtri.ap()),
        (onesR[:], dones.ap()),
    ]
    for (dst, src) in res_list:
        dma("sync", dst, src, dRES)
    for (dst, src) in res_list:
        setw(dst, dRES)
    dma("sync", xT[0][:], x_hbm(0), dX[0])
    if n_tiles > 1:
        dma("sync", xT[1][:], x_hbm(1), dX[1])
    for i in range(3):
        dma("sync", wzB[:, i], wz_hbm(i), dWZ[i])
        dma("gpsimd", wxB[:, i], wx_hbm(i), dWX[i])
        dma("sync", woB[:, i], wo_hbm(i), dWO[i])
        dma("gpsimd", v1B[:, i], v1_hbm(i), dV1[i])
        dma("gpsimd", u2B[:, i], u2_hbm(i), dU2[i])
    dma("gpsimd", u1B[:, 0], u1_hbm(0), dU1[0])
    dma("gpsimd", u1B[:, 1], u1_hbm(1), dU1[1])

    sc(lambda e: e.activation(ArowT[:], alogR[:], AF.Exp),
       rd=[alogR], wr=[ArowT])
    ve(lambda e: e.tensor_scalar_mul(ArowT[:], ArowT[:], -1.0),
       rd=[ArowT], wr=[ArowT])
    pe(lambda e: e.transpose(ps(0, 1, 0, 8), ArowT[:], identR[0:1, 0:1]),
       rd=[ArowT, identR], wr=[ps(0, 1)])
    ve(lambda e: e.tensor_copy(AcolT[:], ps(0, 1, 0, 8)),
       rd=[ps(0, 1)], wr=[AcolT])
    pe(lambda e: e.matmul(ps(512, 8), onesR[0:1, 0:128], dskR[:],
                          start=True, stop=True),
       rd=[onesR, dskR], wr=[ps(512, 8)])
    ve(lambda e: e.tensor_copy(DbcT[:], ps(512, 8)), rd=[ps(512, 8)], wr=[DbcT])
    for h in range(HEADS):
        ve(lambda e, h=h: e.tensor_scalar_mul(DdT[:, h * 128:(h + 1) * 128],
                                              identR[:], DbcT[:, h:h + 1]),
           rd=[identR, DbcT], wr=[DdT])
    ve(lambda e: e.memset(hS[:], 0.0), wr=[hS])
    ve(lambda e: e.memset(statT[:, 47:48], EPS), wr=[statT])

    ln_stats(lambda c: xT[0][:, c, :])
    ln_rstd(RSTD1, NEGM1)
    for c in range(NCH):
        dg_build(c, RSTD1 + c)
        negmr_row(NEGM1 + c, R1ROW + c * 128)


    def chain_sv(c):
        sc(lambda e, c=c: e.activation(statT[:, SPE8:SPE8 + 8],
                                       dtrT[:, c * 8:(c + 1) * 8], AF.Exp),
           rd=[dtrT], wr=[statT])
        sc(lambda e, c=c: e.activation(dtT[:, c * 8:(c + 1) * 8],
                                       statT[:, SPE8:SPE8 + 8], AF.Ln,
                                       bias=onesR[:, 0:1]),
           rd=[statT, onesR], wr=[dtT])

    def chain_pe_a(c):
        pe(lambda e, c=c: e.transpose(ps(1152, 128, 0, 8),
                                      dtT[:, c * 8:(c + 1) * 8], identR[:]),
           rd=[dtT, identR], wr=[ps(1152, 128)])
        ve(lambda e: e.tensor_copy(dtfmT[:], ps(1152, 128, 0, 8)),
           rd=[ps(1152, 128)], wr=[dtfmT])
        ve(lambda e: e.tensor_scalar_mul(ldT[:], dtfmT[:], AcolT[:]),
           rd=[dtfmT, AcolT], wr=[ldT])
        ve(lambda e: e.tensor_tensor_scan(aT[:], ldT[:], onesR[0:8, 0:128],
                                          0.0, ALU.add, ALU.mult),
           rd=[ldT, onesR], wr=[aT])
        sc(lambda e: e.activation(expaT[:], aT[:], AF.Exp), rd=[aT], wr=[expaT])
        sc(lambda e: e.activation(w8T[:], aT[:], AF.Exp, scale=-1.0,
                                  bias=aT[:, 127:128]), rd=[aT], wr=[w8T])
        ve(lambda e: e.tensor_mul(wdt8T[:], w8T[:], dtfmT[:]),
           rd=[w8T, dtfmT], wr=[wdt8T])
        pe(lambda e: e.transpose(ps(1280, 8), aT[:], identR[0:8, 0:8]),
           rd=[aT, identR], wr=[ps(1280, 8)])
        pe(lambda e: e.transpose(ps(1288, 8), wdt8T[:], identR[0:8, 0:8]),
           rd=[wdt8T, identR], wr=[ps(1288, 8)])
        ve(lambda e: e.tensor_copy(atmT[:], ps(1280, 16)),
           rd=[ps(1280, 16)], wr=[atmT])
        dma("sync", AP(arow, 0, [[2048, 1], [128, 8], [1, 128]]), aT[:], dROW[0])
        dma("sync", AP(arow, 1024, [[2048, 1], [128, 8], [1, 128]]), expaT[:],
            dROW[1])

    def chain_pe_b(c):
        for q4 in range(2):
            pe(lambda e, q4=q4: e.matmul(
                ps(q4 * 512, 512), onesR[0:1, 0:128],
                AP(arow, q4 * 512, [[2048, 1], [1, 512]]), start=True, stop=True),
               rd=[onesR, arow], wr=[ps(q4 * 512, 512)])
        pe(lambda e, c=c: e.matmul(ps(1024, 128), BfT[:, c * 128:(c + 1) * 128],
                                   CfT[:, c * 128:(c + 1) * 128],
                                   start=True, stop=True),
           rd=[BfT, CfT], wr=[ps(1024, 128)])
        pe(lambda e, c=c: e.transpose(ps(1296, 64), BfT[:, c * 128:(c + 1) * 128],
                                      identR[0:64, 0:64]),
           rd=[BfT, identR], wr=[ps(1296, 64)])
        pe(lambda e: e.matmul(ps(1408, 8, 0, 64), onesR[0:1, 0:64],
                              AP(arow, 1024 + 127, [[2048, 1], [128, 8]]),
                              start=True, stop=True),
           rd=[onesR, AP(arow, 1024, [[2048, 1], [1, 1024]])], wr=[ps(1408, 8)])

    def chain_v_a(c):
        for h in range(HEADS):
            ve(lambda e, h=h: e.tensor_scalar(
                EminT[:, h * 128:(h + 1) * 128], ps(h * 128, 128),
                atmT[:, h:h + 1], 0.0, ALU.subtract, ALU.min),
               rd=[ps(h * 128, 128), atmT], wr=[EminT])
        ve(lambda e: e.tensor_mul(GTmT[:], ps(1024, 128), triR[:]),
           rd=[ps(1024, 128), triR], wr=[GTmT])
        ve(lambda e: e.tensor_copy(BtmT[:], ps(1296, 64)),
           rd=[ps(1296, 64)], wr=[BtmT])
        ve(lambda e, c=c: e.tensor_copy(e8cT[c][:], ps(1408, 8, 0, 64)),
           rd=[ps(1408, 8)], wr=[e8cT[c]])
        sc(lambda e: e.activation(EminT[:], EminT[:], AF.Exp),
           rd=[EminT], wr=[EminT])

    def chain_v_mt(c):
        for h in range(HEADS):
            ve(lambda e, h=h, c=c: e.scalar_tensor_tensor(
                MTT[:, h * 128:(h + 1) * 128], EminT[:, h * 128:(h + 1) * 128],
                dtT[:, c * 8 + h:c * 8 + h + 1], GTmT[:], ALU.mult, ALU.mult),
               rd=[EminT, dtT, GTmT], wr=[MTT])
        for h in range(HEADS):
            ve(lambda e, h=h: e.tensor_add(MTT[:, h * 128:(h + 1) * 128],
                                           MTT[:, h * 128:(h + 1) * 128],
                                           DdT[:, h * 128:(h + 1) * 128]),
               rd=[MTT, DdT], wr=[MTT])

    def chain_bw(c):
        for h in range(HEADS):
            ve(lambda e, h=h: e.tensor_scalar_mul(
                BwT[:, h * 64:(h + 1) * 64], BtmT[:], atmT[:, 8 + h:9 + h]),
               rd=[BtmT, atmT], wr=[BwT])

    def mtv_pe(c):
        for q4 in range(2):
            pe(lambda e, q4=q4: e.matmul(
                ps(q4 * 512, 512, 0, 64), onesR[0:1, 0:64],
                AP(arow, 1024 + q4 * 512, [[2048, 1], [1, 512]]),
                start=True, stop=True),
               rd=[onesR, AP(arow, 1024, [[2048, 1], [1, 1024]])],
               wr=[ps(q4 * 512, 512)])

    def mtv_v(c):
        for h in range(HEADS):
            ve(lambda e, h=h, c=c: e.tensor_mul(
                MTvT[:, h * 128:(h + 1) * 128], ps(h * 128, 128, 0, 64),
                CfT[:, c * 128:(c + 1) * 128]),
               rd=[ps(h * 128, 128), CfT], wr=[MTvT])

    def y_mms(c):
        for h in range(HEADS):
            for pt in range(2):
                off = 2048 + (2 * h + pt) * 128
                pe(lambda e, h=h, pt=pt, off=off: e.matmul(
                    ps(off, 128),
                    xsT[:, h * 256 + pt * 128:h * 256 + (pt + 1) * 128],
                    MTT[:, h * 128:(h + 1) * 128], start=True, stop=False),
                   rd=[xsT, MTT], wr=[ps(off, 128)])
                pe(lambda e, h=h, pt=pt, off=off: e.matmul(
                    ps(off, 128), hS[:, h * 256 + pt * 128:h * 256 + (pt + 1) * 128],
                    MTvT[:, h * 128:(h + 1) * 128], start=False, stop=True),
                   rd=[hS, MTvT], wr=[ps(off, 128)])

    def gating(c):
        gflat = x1tmT[:].rearrange("p a b -> p (a b)")
        ve(lambda e, c=c, gflat=gflat: e.tensor_tensor(
            gflat.rearrange("p (a b) -> p a b", a=16),
            ps(2048, 2048).rearrange("p (a b) -> p a b", a=16),
            szT[:, :, c * 128:(c + 1) * 128], ALU.mult),
           rd=[ps(2048, 2048), szT], wr=[x1tmT])

    def hinc_mms(c):
        for h in range(HEADS):
            pe(lambda e, h=h: e.matmul(
                ps(2048 + h * 256, 256, 0, 64), BwT[:, h * 64:(h + 1) * 64],
                xsT[:, h * 256:(h + 1) * 256], start=True, stop=True),
               rd=[BwT, xsT], wr=[ps(2048 + h * 256, 256)])

    def hstt(c):
        for h in range(HEADS):
            ve(lambda e, h=h, c=c: e.scalar_tensor_tensor(
                hS[:, h * 256:(h + 1) * 256], hS[:, h * 256:(h + 1) * 256],
                e8cT[c][:, h:h + 1], ps(2048 + h * 256, 256, 0, 64),
                ALU.mult, ALU.add),
               rd=[hS, e8cT[c], ps(2048 + h * 256, 256)], wr=[hS])

    def xs_mms(j, c):
        for k in range(8):
            buf = (16 * j + 8 * c + k) % 3
            for n4 in range(4):
                pe(lambda e, k=k, c=c, n4=n4, buf=buf: e.matmul(
                    ps(2048 + n4 * 512, 512), uT[:, k, c * 128:(c + 1) * 128],
                    wxB[:, buf, n4 * 512:(n4 + 1) * 512],
                    start=(k == 0), stop=False),
                   rd=[uT, wxB[:, buf]], wr=[ps(2048 + n4 * 512, 512)])
            nxt = 16 * j + 8 * c + k + 4
            if nxt <= 16 * n_tiles:
                dma("gpsimd", wxB[:, buf], wx_hbm((nxt - 1) % 8),
                    dWX[(nxt - 1) % 3])
        for n4 in range(4):
            pe(lambda e, n4=n4: e.matmul(
                ps(2048 + n4 * 512, 512), onesR[0:1, 0:128],
                bxR[0:1, n4 * 512:(n4 + 1) * 512], start=False, stop=True),
               rd=[onesR, bxR], wr=[ps(2048 + n4 * 512, 512)])
        ve(lambda e: e.tensor_copy(xsT[:], ps(2048, 2048)),
           rd=[ps(2048, 2048)], wr=[xsT])

    def dt_mms(c):
        off = c * 16
        for k in range(8):
            pe(lambda e, c=c, k=k, off=off: e.matmul(
                ps(off, 8), uT[:, k, c * 128:(c + 1) * 128],
                wdtR[:, k, :], start=(k == 0), stop=False),
               rd=[uT, wdtR], wr=[ps(off, 8)])
        pe(lambda e, off=off: e.matmul(ps(off, 8), onesR[0:2, 0:128], bdtR[:],
                                       start=False, stop=True),
           rd=[onesR, bdtR], wr=[ps(off, 8)])
        ve(lambda e, c=c, off=off: e.tensor_copy(dtrT[:, c * 8:(c + 1) * 8],
                                                 ps(off, 8)),
           rd=[ps(off, 8)], wr=[dtrT])

    def outproj_chunk(j, c, xv):
        for k in range(16):
            buf = (32 * j + 16 * c + k) % 3
            for n2 in range(2):
                pe(lambda e, k=k, n2=n2, buf=buf: e.matmul(
                    ps(1024 + n2 * 512, 512), gT_slice(k),
                    woB[:, buf, n2 * 512:(n2 + 1) * 512],
                    start=(k == 0), stop=False),
                   rd=[x1tmT, woB[:, buf]], wr=[ps(1024 + n2 * 512, 512)])
            nxt = 32 * j + 16 * c + k + 4
            if nxt <= 32 * n_tiles:
                dma("sync", woB[:, buf], wo_hbm((nxt - 1) % 16),
                    dWO[(nxt - 1) % 3])
        for n2 in range(2):
            pe(lambda e, n2=n2: e.matmul(
                ps(1024 + n2 * 512, 512), onesR[0:1, 0:128],
                boR[0:1, n2 * 512:(n2 + 1) * 512], start=False, stop=True),
               rd=[onesR, boR], wr=[ps(1024 + n2 * 512, 512)])
        ve(lambda e, c=c, xv=xv: e.scalar_tensor_tensor(
            rT[:, c, :], xv[:, c, :], 2.0, ps(1024, 1024), ALU.mult, ALU.add),
           rd=[xv, ps(1024, 1024)], wr=[rT])

    def gT_slice(k):
        return AP(x1tmT, k * 128, [[2048, 128], [1, 128]])

    for j in range(n_tiles):
        par = j % 2
        npar = (j + 1) % 2
        xv, xnv = xT[par], xT[npar]

        for c in range(NCH):
            base = c * 1024
            slot = 2 * par + c
            for f in range(8):
                pe(lambda e, c=c, f=f, base=base, slot=slot, xv=xv: e.matmul(
                    ps(base + f * 128, 128), xv[:, c, f * 128:(f + 1) * 128],
                    DgT[:, slot * 128:(slot + 1) * 128], start=True, stop=False),
                   rd=[xv, DgT], wr=[ps(base + f * 128, 128)])
                pe(lambda e, f=f, base=base, slot=slot: e.matmul(
                    ps(base + f * 128, 128), onesR[0:1, 0:128],
                    row_ap(R1ROW + slot * 128), start=False, stop=True),
                   rd=[onesR, rowT], wr=[ps(base + f * 128, 128)])
            for f in range(8):
                ve(lambda e, c=c, f=f, base=base: e.tensor_scalar(
                    uT[:, f, c * 128:(c + 1) * 128], ps(base + f * 128, 128),
                    gamR[:, f:f + 1], betR[:, f:f + 1], ALU.mult, ALU.add),
                   rd=[ps(base + f * 128, 128), gamR, betR], wr=[uT])
        dt_mms(0)
        dt_mms(1)
        chain_sv(0)
        for bc in range(2):
            bank = 4 + bc
            wsl = (0, 64) if bc == 0 else (64, 128)
            for k in range(8):
                pe(lambda e, k=k, bank=bank, wsl=wsl: e.matmul(
                    ps(bank * 512, 256, 0, 64), wbcR[:, k, wsl[0]:wsl[1]],
                    uT[:, k, :], start=(k == 0), stop=(k == 7)),
                   rd=[wbcR, uT], wr=[ps(bank * 512, 256)])
            if bc == 0:
                ve(lambda e, bank=bank: e.tensor_scalar_add(
                    BfT[:], ps(bank * 512, 256, 0, 64), bBR[:]),
                   rd=[ps(bank * 512, 256), bBR], wr=[BfT])
            else:
                ve(lambda e, bank=bank: e.tensor_scalar_add(
                    CfT[:], ps(bank * 512, 256, 0, 64), bCR[:]),
                   rd=[ps(bank * 512, 256), bCR], wr=[CfT])
        for ft in range(16):
            bank = 4 + (ft % 4)
            buf = (16 * j + ft) % 3
            for k in range(8):
                pe(lambda e, ft=ft, k=k, buf=buf, bank=bank: e.matmul(
                    ps(bank * 512, 256), wzB[:, buf, k, :], uT[:, k, :],
                    start=(k == 0), stop=(k == 7)),
                   rd=[wzB[:, buf], uT], wr=[ps(bank * 512, 256)])
            sc(lambda e, ft=ft, bank=bank: e.activation(
                szT[:, ft, :], ps(bank * 512, 256), AF.Silu,
                bias=bzR[:, ft:ft + 1]),
               rd=[ps(bank * 512, 256), bzR], wr=[szT])
            nxtz = 16 * j + ft + 4
            if nxtz <= 16 * n_tiles:
                dma("sync", wzB[:, buf], wz_hbm((nxtz - 1) % 16),
                    dWZ[(nxtz - 1) % 3])
            if ft == 1:
                chain_pe_a(0)
            elif ft == 3:
                chain_pe_b(0)
                chain_v_a(0)
            elif ft == 5:
                mtv_pe(0)
                mtv_v(0)
                chain_v_mt(0)
                chain_bw(0)
            elif ft == 7:
                chain_sv(1)
                chain_pe_a(1)
            elif ft == 9:
                chain_pe_b(1)
                chain_v_a(1)
            elif ft == 11:
                mtv_pe(1)
            elif ft == 99:
                pass
        xs_mms(j, 0)
        y_mms(0)
        gating(0)
        hinc_mms(0)
        hstt(0)
        outproj_chunk(j, 0, xv)
        mtv_v(1)
        chain_v_mt(1)
        chain_bw(1)
        if j + 1 < n_tiles:
            ln_stats(lambda c, xnv=xnv: xnv[:, c, :])
            ln_rstd(RSTD1 + 2 * npar, NEGM1 + 2 * npar)
            for cc in range(NCH):
                dg_build(npar * 2 + cc, RSTD1 + 2 * npar + cc)
                negmr_row(NEGM1 + 2 * npar + cc, R1ROW + (npar * 2 + cc) * 128)
        xs_mms(j, 1)
        y_mms(1)
        gating(1)
        hinc_mms(1)
        hstt(1)
        outproj_chunk(j, 1, xv)
        ln_stats(lambda c: rT[:, c, :])
        ln_rstd(RSTD2, NEGM2)
        for c in range(NCH):
            dg_build(4 + c, RSTD2 + c)
            negmr_row(NEGM2 + c, R2ROW + c * 128)
        for c in range(NCH):
            ve(lambda e, c=c: e.tensor_scalar(
                x1tmT[:, c, :], rT[:, c, :],
                statT[:, RSTD2 + c:RSTD2 + c + 1],
                statT[:, NEGM2 + c:NEGM2 + c + 1], ALU.mult, ALU.add),
               rd=[rT, statT], wr=[x1tmT])
        for c in range(NCH):
            for f in range(8):
                pe(lambda e, c=c, f=f: e.matmul(
                    ps(c * 1024 + f * 128, 128), rT[:, c, f * 128:(f + 1) * 128],
                    DgT[:, (4 + c) * 128:(5 + c) * 128], start=True, stop=False),
                   rd=[rT, DgT], wr=[ps(c * 1024 + f * 128, 128)])
                pe(lambda e, c=c, f=f: e.matmul(
                    ps(c * 1024 + f * 128, 128), onesR[0:1, 0:128],
                    row_ap(R2ROW + c * 128), start=False, stop=True),
                   rd=[onesR, rowT], wr=[ps(c * 1024 + f * 128, 128)])
            ve(lambda e, c=c: e.tensor_copy(
                uT[:, :, c * 128:(c + 1) * 128],
                ps(c * 1024, 1024).rearrange("p (a b) -> p a b", a=8)),
               rd=[ps(c * 1024, 1024)], wr=[uT])
        if j + 2 < n_tiles:
            dma("sync", xv[:], x_hbm(j + 2), dX[par])
        for k in range(8):
            buf = k % 2
            for mt in range(2):
                pe(lambda e, mt=mt, k=k, buf=buf: e.matmul(
                    ps(mt * 512, 256), u1B[:, buf, mt * 128:(mt + 1) * 128],
                    uT[:, k, :], start=(k == 0), stop=(k == 7)),
                   rd=[u1B[:, buf], uT], wr=[ps(mt * 512, 256)])
            nxt = 8 * j + k + 3
            if nxt <= 8 * n_tiles:
                dma("gpsimd", u1B[:, buf], u1_hbm((nxt - 1) % 8),
                    dU1[(nxt - 1) % 2])
        ve(lambda e: e.tensor_copy(PT[:, 0, :], ps(0, 256)),
           rd=[ps(0, 256)], wr=[PT])
        ve(lambda e: e.tensor_copy(PT[:, 1, :], ps(512, 256)),
           rd=[ps(512, 256)], wr=[PT])
        for m in range(32):
            buf = (32 * j + m) % 3
            hoff = 1024 + (m % 2) * 512
            pe(lambda e, m=m, buf=buf, hoff=hoff: e.matmul(
                ps(hoff, 256), v1B[:, buf, 0, :], PT[:, 0, :],
                start=True, stop=False),
               rd=[v1B[:, buf], PT], wr=[ps(hoff, 256)])
            pe(lambda e, m=m, buf=buf, hoff=hoff: e.matmul(
                ps(hoff, 256), v1B[:, buf, 1, :], PT[:, 1, :],
                start=False, stop=True),
               rd=[v1B[:, buf], PT], wr=[ps(hoff, 256)])
            sc(lambda e, m=m, hoff=hoff: e.activation(
                HsT[m % 2][:], ps(hoff, 256), AF.Gelu_apprx_tanh,
                bias=b1R[:, m:m + 1]),
               rd=[ps(hoff, 256), b1R], wr=[HsT[m % 2]])
            for mt in range(2):
                pe(lambda e, m=m, mt=mt, buf=buf: e.matmul(
                    ps(2048 + mt * 512, 256), u2B[:, buf, mt * 128:(mt + 1) * 128],
                    HsT[m % 2][:], start=(m == 0), stop=(m == 31)),
                   rd=[u2B[:, buf], HsT[m % 2]], wr=[ps(2048 + mt * 512, 256)])
            nxtf = 32 * j + m + 4
            if nxtf <= 32 * n_tiles:
                dma("gpsimd", v1B[:, buf], v1_hbm((nxtf - 1) % 32),
                    dV1[(nxtf - 1) % 3])
                dma("gpsimd", u2B[:, buf], u2_hbm((nxtf - 1) % 32),
                    dU2[(nxtf - 1) % 3])
        ve(lambda e: e.tensor_copy(QT[:, 0, :], ps(2048, 256)),
           rd=[ps(2048, 256)], wr=[QT])
        ve(lambda e: e.tensor_copy(QT[:, 1, :], ps(2560, 256)),
           rd=[ps(2560, 256)], wr=[QT])
        for c in range(NCH):
            for k2 in range(2):
                for n2 in range(2):
                    pe(lambda e, c=c, k2=k2, n2=n2: e.matmul(
                        ps(3072 + n2 * 512, 512), QT[:, k2, c * 128:(c + 1) * 128],
                        v2R[:, k2, n2 * 512:(n2 + 1) * 512],
                        start=(k2 == 0), stop=False),
                       rd=[QT, v2R], wr=[ps(3072 + n2 * 512, 512)])
            for n2 in range(2):
                pe(lambda e, n2=n2: e.matmul(
                    ps(3072 + n2 * 512, 512), onesR[0:1, 0:128],
                    b2R[0:1, n2 * 512:(n2 + 1) * 512], start=False, stop=True),
                   rd=[onesR, b2R], wr=[ps(3072 + n2 * 512, 512)])
            ve(lambda e, c=c: e.tensor_add(
                x1tmT[:, c, :], ps(3072, 1024), x1tmT[:, c, :]),
               rd=[ps(3072, 1024), x1tmT], wr=[x1tmT])
        ln_stats(lambda c: x1tmT[:, c, :])
        ln_rstd(RSTD3, NEGM3)
        for c in range(NCH):
            negmr_row(NEGM3 + c, R3ROW + c * 128)
        for c in range(NCH):
            ve(lambda e, c=c: e.tensor_scalar(
                rT[:, c, :], x1tmT[:, c, :],
                statT[:, RSTD3 + c:RSTD3 + c + 1],
                statT[:, NEGM3 + c:NEGM3 + c + 1], ALU.mult, ALU.add),
               rd=[x1tmT, statT], wr=[rT])
        for c in range(NCH):
            dma("sync", AP(dout, j * TILE * D + c * CH * D,
                           [[D, 128], [1, D]]), rT[:, c, :], dOUT)

    fin = dOUT.n

    def final_wait(e, fin=fin):
        e.wait_ge(dOUT.sem, fin)
        return None
    cur["sync"].append(final_wait)

    for bi, blk in enumerate(blocks):
        with nc.Block(f"blk{bi}", no_gpsimd_drain=True) as B:
            for eng in ENGINES:
                fns = blk[eng]
                if not fns:
                    continue
                getattr(B, eng)(lambda E, fns=fns: [f(E) for f in fns])
    ctx.close()
    return nc


def _prep_shared(L0_attn_W_in, L0_attn_b_in, L0_attn_A_log, L0_attn_D_skip,
                 L0_attn_dt_bias, L0_attn_W_out, L0_attn_b_out, L0_attn_ln_gamma,
                 L0_attn_ln_beta, L0_ffn_U1, L0_ffn_V1, L0_ffn_b1, L0_ffn_U2,
                 L0_ffn_V2, L0_ffn_b2):
    W = np.asarray(L0_attn_W_in, np.float32)
    b = np.asarray(L0_attn_b_in, np.float32)
    c = np.ascontiguousarray
    m = {}
    m["wz"] = c(W[:, 0:2048].reshape(8, 128, 16, 128).transpose(2, 1, 0, 3))
    m["wx"] = c(W[:, 2048:4096].reshape(8, 128, 2048))
    m["wbc"] = c(W[:, 4096:4224].reshape(8, 128, 128).transpose(1, 0, 2))
    m["wdt"] = c(W[:, 4224:4232].reshape(8, 128, 8).transpose(1, 0, 2))
    m["wo"] = c(np.asarray(L0_attn_W_out, np.float32).reshape(16, 128, 1024))
    m["u1"] = c(np.asarray(L0_ffn_U1, np.float32).reshape(8, 128, 256))
    m["v1"] = c(np.asarray(L0_ffn_V1, np.float32).reshape(2, 128, 32, 128)
                .transpose(2, 1, 0, 3))
    m["u2"] = c(np.asarray(L0_ffn_U2, np.float32).reshape(32, 128, 256))
    m["v2"] = c(np.asarray(L0_ffn_V2, np.float32).reshape(2, 128, 1024)
                .transpose(1, 0, 2))
    m["bz"] = c(b[0:2048].reshape(16, 128).T)
    m["bx"] = c(b[2048:4096].reshape(1, 2048))
    m["bB"] = c(b[4096:4160].reshape(64, 1))
    m["bC"] = c(b[4160:4224].reshape(64, 1))
    m["bdt"] = c(np.stack([b[4224:4232], np.asarray(L0_attn_dt_bias, np.float32)]))
    m["bo"] = c(np.asarray(L0_attn_b_out, np.float32).reshape(1, 1024))
    m["b1"] = c(np.asarray(L0_ffn_b1, np.float32).reshape(32, 128).T)
    m["b2"] = c(np.asarray(L0_ffn_b2, np.float32).reshape(1, 1024))
    m["gam"] = c(np.asarray(L0_attn_ln_gamma, np.float32).reshape(8, 128).T)
    m["bet"] = c(np.asarray(L0_attn_ln_beta, np.float32).reshape(8, 128).T)
    m["alog"] = c(np.asarray(L0_attn_A_log, np.float32).reshape(1, 8))
    m["dsk"] = c(np.asarray(L0_attn_D_skip, np.float32).reshape(1, 8))
    m["ident"] = np.eye(128, dtype=np.float32)
    m["tri"] = np.triu(np.ones((128, 128), np.float32))
    m["onesc"] = np.ones((128, 128), np.float32)
    return m


def kernel(L0_attn_input_tensor, L0_attn_ln_gamma, L0_attn_ln_beta, L0_attn_W_in,
           L0_attn_b_in, L0_attn_A_log, L0_attn_D_skip, L0_attn_dt_bias,
           L0_attn_W_out, L0_attn_b_out, L0_ffn_U1, L0_ffn_V1, L0_ffn_b1,
           L0_ffn_U2, L0_ffn_V2, L0_ffn_b2):
    x = np.ascontiguousarray(np.asarray(L0_attn_input_tensor, np.float32))
    shared = _prep_shared(L0_attn_W_in, L0_attn_b_in, L0_attn_A_log,
                          L0_attn_D_skip, L0_attn_dt_bias, L0_attn_W_out,
                          L0_attn_b_out, L0_attn_ln_gamma, L0_attn_ln_beta,
                          L0_ffn_U1, L0_ffn_V1, L0_ffn_b1, L0_ffn_U2,
                          L0_ffn_V2, L0_ffn_b2)
    nc = build_nc(SEQ // TILE)
    in_maps = []
    for i in range(8):
        im = dict(shared)
        im["x"] = np.ascontiguousarray(x[i])
        in_maps.append(im)
    res = run_bass_kernel_spmd(nc, in_maps, core_ids=list(range(8)))
    out = np.stack([np.asarray(r["out"], np.float32) for r in res.results])
    return out


# version 08
# speedup vs baseline: 1.3523x; 1.1221x over previous
import numpy as np
from contextlib import ExitStack

import concourse.bass as bass
import concourse.mybir as mybir
from concourse.bass import AP
from concourse.bass_utils import run_bass_kernel_spmd

F32 = mybir.dt.float32
AF = mybir.ActivationFunctionType
ALU = mybir.AluOpType
AX = mybir.AxisListType

D = 1024
SEQ = 4096
HEADS = 8
TILE = 256
CH = 128
NCH = TILE // CH
EPS = 1e-5

ENGINES = ("tensor", "vector", "scalar", "sync", "gpsimd")
DMA_Q = ("sync", "gpsimd")
GRANULE = {"wzB": 1024, "wxB": 2048, "woB": 1024, "v1B": 256, "u2B": 256,
           "u1B": 256, "arow": 1024}


def build_nc(n_tiles=16):
    nc = bass.Bass("TRN2", target_bir_lowering=False)
    ctx = ExitStack()

    dx = nc.dram_tensor("x", [SEQ, D], F32, kind="ExternalInput")
    dout = nc.dram_tensor("out", [SEQ, D], F32, kind="ExternalOutput")
    dwz = nc.dram_tensor("wz", [16, 128, 8, 128], F32, kind="ExternalInput")
    dwx = nc.dram_tensor("wx", [8, 128, 2048], F32, kind="ExternalInput")
    dwbc = nc.dram_tensor("wbc", [128, 8, 128], F32, kind="ExternalInput")
    dwdt = nc.dram_tensor("wdt", [128, 8, 8], F32, kind="ExternalInput")
    dwo = nc.dram_tensor("wo", [16, 128, 1024], F32, kind="ExternalInput")
    du1 = nc.dram_tensor("u1", [8, 128, 256], F32, kind="ExternalInput")
    dv1 = nc.dram_tensor("v1", [32, 128, 2, 128], F32, kind="ExternalInput")
    du2 = nc.dram_tensor("u2", [32, 128, 256], F32, kind="ExternalInput")
    dv2 = nc.dram_tensor("v2", [128, 2, 1024], F32, kind="ExternalInput")
    dbz = nc.dram_tensor("bz", [128, 16], F32, kind="ExternalInput")
    dbx = nc.dram_tensor("bx", [1, 2048], F32, kind="ExternalInput")
    dbB = nc.dram_tensor("bB", [64, 1], F32, kind="ExternalInput")
    dbC = nc.dram_tensor("bC", [64, 1], F32, kind="ExternalInput")
    dbdt = nc.dram_tensor("bdt", [2, 8], F32, kind="ExternalInput")
    dbo = nc.dram_tensor("bo", [1, 1024], F32, kind="ExternalInput")
    db1 = nc.dram_tensor("b1", [128, 32], F32, kind="ExternalInput")
    db2 = nc.dram_tensor("b2", [1, 1024], F32, kind="ExternalInput")
    dgam = nc.dram_tensor("gam", [128, 8], F32, kind="ExternalInput")
    dbet = nc.dram_tensor("bet", [128, 8], F32, kind="ExternalInput")
    dalog = nc.dram_tensor("alog", [1, 8], F32, kind="ExternalInput")
    ddsk = nc.dram_tensor("dsk", [1, 8], F32, kind="ExternalInput")
    dident = nc.dram_tensor("ident", [128, 128], F32, kind="ExternalInput")
    dtri = nc.dram_tensor("tri", [128, 128], F32, kind="ExternalInput")
    dones = nc.dram_tensor("onesc", [128, 128], F32, kind="ExternalInput")

    sb = lambda name, p, f: ctx.enter_context(nc.sbuf_tensor(name, [p, f], F32))
    sb3 = lambda name, p, a, b: ctx.enter_context(nc.sbuf_tensor(name, [p, a, b], F32))
    sb4 = lambda name, p, a, b, c: ctx.enter_context(
        nc.sbuf_tensor(name, [p, a, b, c], F32))

    xT = [sb3("xT0", 128, NCH, 1024), sb3("xT1", 128, NCH, 1024)]
    uT = sb3("uT", 128, 8, TILE)
    szT = sb3("szT", 128, 16, TILE)
    xsT = sb("xsT", 128, 2048)
    BfT = sb("BfT", 64, TILE)
    CfT = sb("CfT", 64, TILE)
    dtrT = sb("dtrT", 128, NCH * 8)
    dtT = sb("dtT", 128, NCH * 8)
    dtfmT = sb("dtfmT", 8, 128)
    ldT = sb("ldT", 8, 128)
    w8T = sb("w8T", 8, 128)
    aT = sb("aT", 8, 128)
    wdt8T = sb("wdt8T", 8, 128)
    expaT = sb("expaT", 8, 128)
    atmT = sb("atmT", 128, 16)
    arow = sb("arow", 1, 2048)
    e8cT = [sb("e8c0", 64, 8), sb("e8c1", 64, 8)]
    BtmT = sb("BtmT", 128, 64)
    BwT = sb("BwT", 128, 512)
    EminT = sb("EminT", 128, 1024)
    GTmT = sb("GTmT", 128, 128)
    MTT = sb("MTT", 128, 1024)
    MTvT = sb("MTvT", 64, 1024)
    hS = sb("hS", 64, 2048)
    rT = sb3("rT", 128, NCH, 1024)
    x1tmT = sb3("x1tm", 128, NCH, 1024)
    PT = sb3("PT", 128, 2, TILE)
    HsT = [sb("Hs0", 128, TILE), sb("Hs1", 128, TILE)]
    QT = sb3("QT", 128, 2, TILE)
    statT = sb("statT", 128, 48)
    rowT = sb("rowT", 1, 1024)
    DgT = sb("DgT", 128, 768)
    DdT = sb("DdT", 128, 1024)
    DbcT = sb("DbcT", 128, 8)
    AcolT = sb("AcolT", 8, 1)
    ArowT = sb("ArowT", 1, 8)

    wzB = sb4("wzB", 128, 3, 8, 128)
    wxB = sb3("wxB", 128, 3, 2048)
    woB = sb3("woB", 128, 3, 1024)
    v1B = sb4("v1B", 128, 3, 2, 128)
    u2B = sb3("u2B", 128, 3, 256)
    u1B = sb3("u1B", 128, 2, 256)
    v2R = sb3("v2R", 128, 2, 1024)
    wbcR = sb3("wbcR", 128, 8, 128)
    wdtR = sb3("wdtR", 128, 8, 8)
    bzR = sb("bzR", 128, 16)
    bxR = sb("bxR", 1, 2048)
    bBR = sb("bBR", 64, 1)
    bCR = sb("bCR", 64, 1)
    bdtR = sb("bdtR", 2, 8)
    boR = sb("boR", 1, 1024)
    b1R = sb("b1R", 128, 32)
    b2R = sb("b2R", 1, 1024)
    gamR = sb("gamR", 128, 8)
    betR = sb("betR", 128, 8)
    alogR = sb("alogR", 1, 8)
    dskR = sb("dskR", 1, 8)
    identR = sb("identR", 128, 128)
    triR = sb("triR", 128, 128)
    onesR = sb("onesR", 128, 128)

    PS = ctx.enter_context(nc.psum_tensor("PS", [128, 4096], F32))

    def ps(lo, n, p0=0, pn=128):
        return PS[p0:p0 + pn, lo:lo + n]

    class SemC:
        def __init__(self, name):
            self.sem = ctx.enter_context(nc.semaphore(name))
            self.n = 0

    sPE, sV, sS = SemC("sPE"), SemC("sV"), SemC("sS")
    dX = [SemC("dXa"), SemC("dXb")]
    dWZ = [SemC("dWZ%d" % i) for i in range(3)]
    dWX = [SemC("dWX%d" % i) for i in range(3)]
    dWO = [SemC("dWO%d" % i) for i in range(3)]
    dU1 = [SemC("dU1a"), SemC("dU1b")]
    dV1 = [SemC("dV1%d" % i) for i in range(3)]
    dU2 = [SemC("dU2%d" % i) for i in range(3)]
    dRES, dOUT = SemC("dRES"), SemC("dOUT")
    dROW = [SemC("dROWa"), SemC("dROWb")]
    OWN = {"tensor": sPE, "vector": sV, "scalar": sS}

    blocks = []
    cur = {}
    dirty = {"vector": set(), "scalar": set()}
    waited = {}
    lastw = {}
    readers = {}

    def new_block():
        nonlocal cur
        cur = {e: [] for e in ENGINES}
        blocks.append(cur)
        for s in dirty.values():
            s.clear()

    def keys_of(ap):
        t = getattr(ap, "tensor", ap)
        name = getattr(t, "name", None)
        off = getattr(ap, "offset", 0)
        if name == "PS":
            span = 1
            for stp, cnt in list(ap.ap)[1:]:
                span += (cnt - 1) * stp
            return [("PS", b) for b in range(off // 512, (off + span - 1) // 512 + 1)]
        g = GRANULE.get(name)
        if g is not None:
            return [(name, off // g)]
        return [name]

    def step(eng, emit, rd=(), wr=(), extra=(), inc=None, inc_amt=1):
        rdk = [k for a in rd for k in keys_of(a)]
        wrk = [k for a in wr for k in keys_of(a)]
        want = {}

        def need(dep):
            deng, dsem, dval = dep
            if deng == eng and deng not in DMA_Q:
                return
            cur_ = want.get(id(dsem))
            if cur_ is None or cur_[1] < dval:
                want[id(dsem)] = (dsem, dval)

        for k in rdk:
            w = lastw.get(k)
            if w:
                need(w)
        for k in wrk:
            w = lastw.get(k)
            if w:
                need(w)
            for dep in readers.get(k, {}).values():
                need(dep)
        for (s, v) in extra:
            need(("_x", s, v))
        if inc is None:
            inc = OWN.get(eng)
        inc.n += inc_amt
        me = (eng, inc, inc.n)
        for k in rdk:
            readers.setdefault(k, {})[eng] = me
        for k in wrk:
            lastw[k] = me
            readers[k] = {}
        nd = False
        if eng in dirty:
            if (set(rdk) | set(wrk)) & dirty[eng]:
                nd = True
                dirty[eng].clear()
            dirty[eng].update(wrk)
        wl = []
        for (dsem, dval) in want.values():
            key = (eng, id(dsem))
            if waited.get(key, -1) < dval:
                waited[key] = dval
                wl.append((dsem.sem, dval))

        def fn(e, wl=wl, emit=emit, inc=inc, amt=inc_amt, nd=nd):
            if nd:
                e.drain()
            for (sem, v) in wl:
                e.wait_ge(sem, v)
            ins = emit(e)
            if ins is not None:
                ins.then_inc(inc.sem, amt)
            return ins

        cur[eng].append(fn)

    def pe(emit, rd=(), wr=()):
        step("tensor", emit, rd=rd, wr=wr)

    def ve(emit, rd=(), wr=()):
        step("vector", emit, rd=rd, wr=wr)

    def sc(emit, rd=(), wr=()):
        step("scalar", emit, rd=rd, wr=wr)

    def dma(queue, out_ap, in_ap, sem):
        step(queue, lambda e, o=out_ap, i=in_ap: e.dma_start(out=o, in_=i),
             rd=[in_ap], wr=[out_ap], inc=sem, inc_amt=16)

    def setw(buf_ap, sem):
        for k in keys_of(buf_ap):
            lastw[k] = ("sync", sem, sem.n)

    def x_hbm(j):
        return AP(dx, j * TILE * D, [[D, 128], [CH * D, NCH], [1, D]])

    def wz_hbm(ft):
        return AP(dwz, ft * 128 * 8 * 128, [[1024, 128], [128, 8], [1, 128]])

    def wx_hbm(k):
        return AP(dwx, k * 128 * 2048, [[2048, 128], [1, 2048]])

    def wo_hbm(k):
        return AP(dwo, k * 128 * 1024, [[1024, 128], [1, 1024]])

    def v1_hbm(m):
        return AP(dv1, m * 128 * 2 * 128, [[256, 128], [128, 2], [1, 128]])

    def u2_hbm(m):
        return AP(du2, m * 128 * 256, [[256, 128], [1, 256]])

    def u1_hbm(k):
        return AP(du1, k * 128 * 256, [[256, 128], [1, 256]])

    RSTD1, NEGM1 = 0, 4
    SSUM, SSSQ, SVAR, SLNV, SMR = 8, 10, 12, 14, 16
    RSTD2, NEGM2, RSTD3, NEGM3 = 28, 30, 32, 34
    SPE8 = 36
    R1ROW, R2ROW, R3ROW = 0, 512, 768

    def ln_stats(src):
        for c in range(NCH):
            ve(lambda e, c=c, src=src: e.reduce_sum(
                statT[:, SSUM + c:SSUM + c + 1], src(c), axis=AX.X),
               rd=[src(c)], wr=[statT])
            sc(lambda e, c=c, src=src: e.activation(
                EminT[:], src(c), AF.Square,
                accum_out=statT[:, SSSQ + c:SSSQ + c + 1]),
               rd=[src(c)], wr=[EminT, statT])

    def ln_rstd(rcol, ncol):
        ve(lambda e: e.tensor_scalar_mul(
            statT[:, SMR:SMR + 2], statT[:, SSUM:SSUM + 2], 1.0 / D),
           rd=[statT], wr=[statT])
        ve(lambda e: e.tensor_scalar_mul(
            statT[:, SVAR:SVAR + 2], statT[:, SSSQ:SSSQ + 2], 1.0 / D),
           rd=[statT], wr=[statT])
        ve(lambda e: e.tensor_mul(statT[:, SLNV:SLNV + 2], statT[:, SMR:SMR + 2],
                                  statT[:, SMR:SMR + 2]), rd=[statT], wr=[statT])
        ve(lambda e: e.tensor_sub(statT[:, SVAR:SVAR + 2], statT[:, SVAR:SVAR + 2],
                                  statT[:, SLNV:SLNV + 2]), rd=[statT], wr=[statT])
        sc(lambda e: e.activation(statT[:, SLNV:SLNV + 2], statT[:, SVAR:SVAR + 2],
                                  AF.Ln, bias=statT[:, 47:48]),
           rd=[statT], wr=[statT])
        sc(lambda e, rcol=rcol: e.activation(
            statT[:, rcol:rcol + 2], statT[:, SLNV:SLNV + 2], AF.Exp, scale=-0.5),
           rd=[statT], wr=[statT])
        ve(lambda e, rcol=rcol: e.tensor_mul(
            statT[:, SMR:SMR + 2], statT[:, SMR:SMR + 2], statT[:, rcol:rcol + 2]),
           rd=[statT], wr=[statT])
        ve(lambda e, ncol=ncol: e.tensor_scalar_mul(
            statT[:, ncol:ncol + 2], statT[:, SMR:SMR + 2], -1.0),
           rd=[statT], wr=[statT])

    def negmr_row(col, row_off):
        pe(lambda e, col=col: e.transpose(ps(3968, 128, 0, 1),
                                          statT[:, col:col + 1], identR[:]),
           rd=[statT, identR], wr=[ps(3968, 128)])
        ve(lambda e, ro=row_off: e.tensor_copy(
            AP(rowT, ro, [[1024, 1], [1, 128]]), ps(3968, 128, 0, 1)),
           rd=[ps(3968, 128)], wr=[rowT])

    def dg_build(slot, col):
        ve(lambda e, slot=slot, col=col: e.tensor_scalar_mul(
            DgT[:, slot * 128:(slot + 1) * 128], identR[:], statT[:, col:col + 1]),
           rd=[identR, statT], wr=[DgT])

    def row_ap(off):
        return AP(rowT, off, [[1024, 1], [1, 128]])

    new_block()
    res_list = [
        (v2R[:], dv2.ap()), (wbcR[:], dwbc.ap()),
        (wdtR[:], dwdt.ap()), (bzR[:], dbz.ap()), (bxR[:], dbx.ap()),
        (bBR[:], dbB.ap()), (bCR[:], dbC.ap()), (bdtR[:], dbdt.ap()),
        (boR[:], dbo.ap()), (b1R[:], db1.ap()), (b2R[:], db2.ap()),
        (gamR[:], dgam.ap()), (betR[:], dbet.ap()), (alogR[:], dalog.ap()),
        (dskR[:], ddsk.ap()), (identR[:], dident.ap()), (triR[:], dtri.ap()),
        (onesR[:], dones.ap()),
    ]
    for (dst, src) in res_list:
        dma("sync", dst, src, dRES)
    for (dst, src) in res_list:
        setw(dst, dRES)
    dma("sync", xT[0][:], x_hbm(0), dX[0])
    if n_tiles > 1:
        dma("sync", xT[1][:], x_hbm(1), dX[1])
    for i in range(3):
        dma("sync", wzB[:, i], wz_hbm(i), dWZ[i])
        dma("gpsimd", wxB[:, i], wx_hbm(i), dWX[i])
        dma("sync", woB[:, i], wo_hbm(i), dWO[i])
        dma("gpsimd", v1B[:, i], v1_hbm(i), dV1[i])
        dma("gpsimd", u2B[:, i], u2_hbm(i), dU2[i])
    dma("gpsimd", u1B[:, 0], u1_hbm(0), dU1[0])
    dma("gpsimd", u1B[:, 1], u1_hbm(1), dU1[1])

    sc(lambda e: e.activation(ArowT[:], alogR[:], AF.Exp),
       rd=[alogR], wr=[ArowT])
    ve(lambda e: e.tensor_scalar_mul(ArowT[:], ArowT[:], -1.0),
       rd=[ArowT], wr=[ArowT])
    pe(lambda e: e.transpose(ps(0, 1, 0, 8), ArowT[:], identR[0:1, 0:1]),
       rd=[ArowT, identR], wr=[ps(0, 1)])
    ve(lambda e: e.tensor_copy(AcolT[:], ps(0, 1, 0, 8)),
       rd=[ps(0, 1)], wr=[AcolT])
    pe(lambda e: e.matmul(ps(512, 8), onesR[0:1, 0:128], dskR[:],
                          start=True, stop=True),
       rd=[onesR, dskR], wr=[ps(512, 8)])
    ve(lambda e: e.tensor_copy(DbcT[:], ps(512, 8)), rd=[ps(512, 8)], wr=[DbcT])
    for h in range(HEADS):
        ve(lambda e, h=h: e.tensor_scalar_mul(DdT[:, h * 128:(h + 1) * 128],
                                              identR[:], DbcT[:, h:h + 1]),
           rd=[identR, DbcT], wr=[DdT])
    ve(lambda e: e.memset(hS[:], 0.0), wr=[hS])
    ve(lambda e: e.memset(statT[:, 47:48], EPS), wr=[statT])

    ln_stats(lambda c: xT[0][:, c, :])
    ln_rstd(RSTD1, NEGM1)
    for c in range(NCH):
        dg_build(c, RSTD1 + c)
        negmr_row(NEGM1 + c, R1ROW + c * 128)


    def chain_sv(c):
        sc(lambda e, c=c: e.activation(statT[:, SPE8:SPE8 + 8],
                                       dtrT[:, c * 8:(c + 1) * 8], AF.Exp),
           rd=[dtrT], wr=[statT])
        sc(lambda e, c=c: e.activation(dtT[:, c * 8:(c + 1) * 8],
                                       statT[:, SPE8:SPE8 + 8], AF.Ln,
                                       bias=onesR[:, 0:1]),
           rd=[statT, onesR], wr=[dtT])

    def chain_pe_a(c):
        pe(lambda e, c=c: e.transpose(ps(1152, 128, 0, 8),
                                      dtT[:, c * 8:(c + 1) * 8], identR[:]),
           rd=[dtT, identR], wr=[ps(1152, 128)])
        ve(lambda e: e.tensor_copy(dtfmT[:], ps(1152, 128, 0, 8)),
           rd=[ps(1152, 128)], wr=[dtfmT])
        ve(lambda e: e.tensor_scalar_mul(ldT[:], dtfmT[:], AcolT[:]),
           rd=[dtfmT, AcolT], wr=[ldT])
        ve(lambda e: e.tensor_tensor_scan(aT[:], ldT[:], onesR[0:8, 0:128],
                                          0.0, ALU.add, ALU.mult),
           rd=[ldT, onesR], wr=[aT])
        sc(lambda e: e.activation(expaT[:], aT[:], AF.Exp), rd=[aT], wr=[expaT])
        sc(lambda e: e.activation(w8T[:], aT[:], AF.Exp, scale=-1.0,
                                  bias=aT[:, 127:128]), rd=[aT], wr=[w8T])
        ve(lambda e: e.tensor_mul(wdt8T[:], w8T[:], dtfmT[:]),
           rd=[w8T, dtfmT], wr=[wdt8T])
        pe(lambda e: e.transpose(ps(1280, 8), aT[:], identR[0:8, 0:8]),
           rd=[aT, identR], wr=[ps(1280, 8)])
        pe(lambda e: e.transpose(ps(1288, 8), wdt8T[:], identR[0:8, 0:8]),
           rd=[wdt8T, identR], wr=[ps(1288, 8)])
        ve(lambda e: e.tensor_copy(atmT[:], ps(1280, 16)),
           rd=[ps(1280, 16)], wr=[atmT])
        dma("sync", AP(arow, 0, [[2048, 1], [128, 8], [1, 128]]), aT[:], dROW[0])
        dma("sync", AP(arow, 1024, [[2048, 1], [128, 8], [1, 128]]), expaT[:],
            dROW[1])

    def chain_pe_b(c):
        for q4 in range(2):
            pe(lambda e, q4=q4: e.matmul(
                ps(q4 * 512, 512), onesR[0:1, 0:128],
                AP(arow, q4 * 512, [[2048, 1], [1, 512]]), start=True, stop=True),
               rd=[onesR, arow], wr=[ps(q4 * 512, 512)])
        pe(lambda e, c=c: e.matmul(ps(1024, 128), BfT[:, c * 128:(c + 1) * 128],
                                   CfT[:, c * 128:(c + 1) * 128],
                                   start=True, stop=True),
           rd=[BfT, CfT], wr=[ps(1024, 128)])
        pe(lambda e, c=c: e.transpose(ps(1296, 64), BfT[:, c * 128:(c + 1) * 128],
                                      identR[0:64, 0:64]),
           rd=[BfT, identR], wr=[ps(1296, 64)])
        pe(lambda e: e.matmul(ps(1408, 8, 0, 64), onesR[0:1, 0:64],
                              AP(arow, 1024 + 127, [[2048, 1], [128, 8]]),
                              start=True, stop=True),
           rd=[onesR, AP(arow, 1024, [[2048, 1], [1, 1024]])], wr=[ps(1408, 8)])

    def chain_v_a(c):
        for h in range(HEADS):
            ve(lambda e, h=h: e.tensor_scalar(
                EminT[:, h * 128:(h + 1) * 128], ps(h * 128, 128),
                atmT[:, h:h + 1], 0.0, ALU.subtract, ALU.min),
               rd=[ps(h * 128, 128), atmT], wr=[EminT])
        ve(lambda e: e.tensor_mul(GTmT[:], ps(1024, 128), triR[:]),
           rd=[ps(1024, 128), triR], wr=[GTmT])
        ve(lambda e: e.tensor_copy(BtmT[:], ps(1296, 64)),
           rd=[ps(1296, 64)], wr=[BtmT])
        ve(lambda e, c=c: e.tensor_copy(e8cT[c][:], ps(1408, 8, 0, 64)),
           rd=[ps(1408, 8)], wr=[e8cT[c]])
        sc(lambda e: e.activation(EminT[:], EminT[:], AF.Exp),
           rd=[EminT], wr=[EminT])

    def chain_v_mt(c):
        for h in range(HEADS):
            ve(lambda e, h=h, c=c: e.scalar_tensor_tensor(
                MTT[:, h * 128:(h + 1) * 128], EminT[:, h * 128:(h + 1) * 128],
                dtT[:, c * 8 + h:c * 8 + h + 1], GTmT[:], ALU.mult, ALU.mult),
               rd=[EminT, dtT, GTmT], wr=[MTT])
        for h in range(HEADS):
            ve(lambda e, h=h: e.tensor_add(MTT[:, h * 128:(h + 1) * 128],
                                           MTT[:, h * 128:(h + 1) * 128],
                                           DdT[:, h * 128:(h + 1) * 128]),
               rd=[MTT, DdT], wr=[MTT])

    def chain_bw(c):
        for h in range(HEADS):
            ve(lambda e, h=h: e.tensor_scalar_mul(
                BwT[:, h * 64:(h + 1) * 64], BtmT[:], atmT[:, 8 + h:9 + h]),
               rd=[BtmT, atmT], wr=[BwT])

    def mtv_pe(c):
        for q4 in range(2):
            pe(lambda e, q4=q4: e.matmul(
                ps(q4 * 512, 512, 0, 64), onesR[0:1, 0:64],
                AP(arow, 1024 + q4 * 512, [[2048, 1], [1, 512]]),
                start=True, stop=True),
               rd=[onesR, AP(arow, 1024, [[2048, 1], [1, 1024]])],
               wr=[ps(q4 * 512, 512)])

    def mtv_v(c):
        for h in range(HEADS):
            ve(lambda e, h=h, c=c: e.tensor_mul(
                MTvT[:, h * 128:(h + 1) * 128], ps(h * 128, 128, 0, 64),
                CfT[:, c * 128:(c + 1) * 128]),
               rd=[ps(h * 128, 128), CfT], wr=[MTvT])

    def y_mms(c):
        for h in range(HEADS):
            for pt in range(2):
                off = 2048 + (2 * h + pt) * 128
                pe(lambda e, h=h, pt=pt, off=off: e.matmul(
                    ps(off, 128),
                    xsT[:, h * 256 + pt * 128:h * 256 + (pt + 1) * 128],
                    MTT[:, h * 128:(h + 1) * 128], start=True, stop=False),
                   rd=[xsT, MTT], wr=[ps(off, 128)])
                pe(lambda e, h=h, pt=pt, off=off: e.matmul(
                    ps(off, 128), hS[:, h * 256 + pt * 128:h * 256 + (pt + 1) * 128],
                    MTvT[:, h * 128:(h + 1) * 128], start=False, stop=True),
                   rd=[hS, MTvT], wr=[ps(off, 128)])

    def gating(c):
        gflat = x1tmT[:].rearrange("p a b -> p (a b)")
        ve(lambda e, c=c, gflat=gflat: e.tensor_tensor(
            gflat.rearrange("p (a b) -> p a b", a=16),
            ps(2048, 2048).rearrange("p (a b) -> p a b", a=16),
            szT[:, :, c * 128:(c + 1) * 128], ALU.mult),
           rd=[ps(2048, 2048), szT], wr=[x1tmT])

    def hinc_mms(c):
        for h in range(HEADS):
            pe(lambda e, h=h: e.matmul(
                ps(2048 + h * 256, 256, 0, 64), BwT[:, h * 64:(h + 1) * 64],
                xsT[:, h * 256:(h + 1) * 256], start=True, stop=True),
               rd=[BwT, xsT], wr=[ps(2048 + h * 256, 256)])

    def hstt(c):
        for h in range(HEADS):
            ve(lambda e, h=h, c=c: e.scalar_tensor_tensor(
                hS[:, h * 256:(h + 1) * 256], hS[:, h * 256:(h + 1) * 256],
                e8cT[c][:, h:h + 1], ps(2048 + h * 256, 256, 0, 64),
                ALU.mult, ALU.add),
               rd=[hS, e8cT[c], ps(2048 + h * 256, 256)], wr=[hS])

    def xs_mms(j, c):
        for k in range(8):
            buf = (16 * j + 8 * c + k) % 3
            for n4 in range(4):
                pe(lambda e, k=k, c=c, n4=n4, buf=buf: e.matmul(
                    ps(2048 + n4 * 512, 512), uT[:, k, c * 128:(c + 1) * 128],
                    wxB[:, buf, n4 * 512:(n4 + 1) * 512],
                    start=(k == 0), stop=False),
                   rd=[uT, wxB[:, buf]], wr=[ps(2048 + n4 * 512, 512)])
            nxt = 16 * j + 8 * c + k + 4
            if nxt <= 16 * n_tiles:
                dma("gpsimd", wxB[:, buf], wx_hbm((nxt - 1) % 8),
                    dWX[(nxt - 1) % 3])
        for n4 in range(4):
            pe(lambda e, n4=n4: e.matmul(
                ps(2048 + n4 * 512, 512), onesR[0:1, 0:128],
                bxR[0:1, n4 * 512:(n4 + 1) * 512], start=False, stop=True),
               rd=[onesR, bxR], wr=[ps(2048 + n4 * 512, 512)])
        ve(lambda e: e.tensor_copy(xsT[:], ps(2048, 2048)),
           rd=[ps(2048, 2048)], wr=[xsT])

    def dt_mms(c):
        off = c * 16
        for k in range(8):
            pe(lambda e, c=c, k=k, off=off: e.matmul(
                ps(off, 8), uT[:, k, c * 128:(c + 1) * 128],
                wdtR[:, k, :], start=(k == 0), stop=False),
               rd=[uT, wdtR], wr=[ps(off, 8)])
        pe(lambda e, off=off: e.matmul(ps(off, 8), onesR[0:2, 0:128], bdtR[:],
                                       start=False, stop=True),
           rd=[onesR, bdtR], wr=[ps(off, 8)])
        ve(lambda e, c=c, off=off: e.tensor_copy(dtrT[:, c * 8:(c + 1) * 8],
                                                 ps(off, 8)),
           rd=[ps(off, 8)], wr=[dtrT])

    def outproj_chunk(j, c, xv):
        for k in range(16):
            buf = (32 * j + 16 * c + k) % 3
            for n2 in range(2):
                pe(lambda e, k=k, n2=n2, buf=buf: e.matmul(
                    ps(1024 + n2 * 512, 512), gT_slice(k),
                    woB[:, buf, n2 * 512:(n2 + 1) * 512],
                    start=(k == 0), stop=False),
                   rd=[x1tmT, woB[:, buf]], wr=[ps(1024 + n2 * 512, 512)])
            nxt = 32 * j + 16 * c + k + 4
            if nxt <= 32 * n_tiles:
                dma("sync", woB[:, buf], wo_hbm((nxt - 1) % 16),
                    dWO[(nxt - 1) % 3])
        for n2 in range(2):
            pe(lambda e, n2=n2: e.matmul(
                ps(1024 + n2 * 512, 512), onesR[0:1, 0:128],
                boR[0:1, n2 * 512:(n2 + 1) * 512], start=False, stop=True),
               rd=[onesR, boR], wr=[ps(1024 + n2 * 512, 512)])
        ve(lambda e, c=c, xv=xv: e.scalar_tensor_tensor(
            rT[:, c, :], xv[:, c, :], 2.0, ps(1024, 1024), ALU.mult, ALU.add),
           rd=[xv, ps(1024, 1024)], wr=[rT])

    def gT_slice(k):
        return AP(x1tmT, k * 128, [[2048, 128], [1, 128]])

    for j in range(n_tiles):
        par = j % 2
        npar = (j + 1) % 2
        xv, xnv = xT[par], xT[npar]

        for c in range(NCH):
            base = c * 1024
            slot = 2 * par + c
            for f in range(8):
                pe(lambda e, c=c, f=f, base=base, slot=slot, xv=xv: e.matmul(
                    ps(base + f * 128, 128), xv[:, c, f * 128:(f + 1) * 128],
                    DgT[:, slot * 128:(slot + 1) * 128], start=True, stop=False),
                   rd=[xv, DgT], wr=[ps(base + f * 128, 128)])
                pe(lambda e, f=f, base=base, slot=slot: e.matmul(
                    ps(base + f * 128, 128), onesR[0:1, 0:128],
                    row_ap(R1ROW + slot * 128), start=False, stop=True),
                   rd=[onesR, rowT], wr=[ps(base + f * 128, 128)])
            for f in range(8):
                ve(lambda e, c=c, f=f, base=base: e.tensor_scalar(
                    uT[:, f, c * 128:(c + 1) * 128], ps(base + f * 128, 128),
                    gamR[:, f:f + 1], betR[:, f:f + 1], ALU.mult, ALU.add),
                   rd=[ps(base + f * 128, 128), gamR, betR], wr=[uT])
        dt_mms(0)
        dt_mms(1)
        chain_sv(0)
        for bc in range(2):
            bank = 4 + bc
            wsl = (0, 64) if bc == 0 else (64, 128)
            for k in range(8):
                pe(lambda e, k=k, bank=bank, wsl=wsl: e.matmul(
                    ps(bank * 512, 256, 0, 64), wbcR[:, k, wsl[0]:wsl[1]],
                    uT[:, k, :], start=(k == 0), stop=(k == 7)),
                   rd=[wbcR, uT], wr=[ps(bank * 512, 256)])
            if bc == 0:
                ve(lambda e, bank=bank: e.tensor_scalar_add(
                    BfT[:], ps(bank * 512, 256, 0, 64), bBR[:]),
                   rd=[ps(bank * 512, 256), bBR], wr=[BfT])
            else:
                ve(lambda e, bank=bank: e.tensor_scalar_add(
                    CfT[:], ps(bank * 512, 256, 0, 64), bCR[:]),
                   rd=[ps(bank * 512, 256), bCR], wr=[CfT])
        for ft in range(16):
            bank = 4 + (ft % 4)
            buf = (16 * j + ft) % 3
            for k in range(8):
                pe(lambda e, ft=ft, k=k, buf=buf, bank=bank: e.matmul(
                    ps(bank * 512, 256), wzB[:, buf, k, :], uT[:, k, :],
                    start=(k == 0), stop=(k == 7)),
                   rd=[wzB[:, buf], uT], wr=[ps(bank * 512, 256)])
            sc(lambda e, ft=ft, bank=bank: e.activation(
                szT[:, ft, :], ps(bank * 512, 256), AF.Silu,
                bias=bzR[:, ft:ft + 1]),
               rd=[ps(bank * 512, 256), bzR], wr=[szT])
            nxtz = 16 * j + ft + 4
            if nxtz <= 16 * n_tiles:
                dma("sync", wzB[:, buf], wz_hbm((nxtz - 1) % 16),
                    dWZ[(nxtz - 1) % 3])
            if ft == 1:
                chain_pe_a(0)
            elif ft == 3:
                chain_pe_b(0)
                chain_v_a(0)
            elif ft == 5:
                mtv_pe(0)
                mtv_v(0)
                chain_v_mt(0)
                chain_bw(0)
            elif ft == 7:
                chain_sv(1)
                chain_pe_a(1)
            elif ft == 9:
                chain_pe_b(1)
                chain_v_a(1)
            elif ft == 11:
                mtv_pe(1)
            elif ft == 99:
                pass
        xs_mms(j, 0)
        y_mms(0)
        gating(0)
        hinc_mms(0)
        hstt(0)
        outproj_chunk(j, 0, xv)
        mtv_v(1)
        chain_v_mt(1)
        chain_bw(1)
        if j + 1 < n_tiles:
            ln_stats(lambda c, xnv=xnv: xnv[:, c, :])
            ln_rstd(RSTD1 + 2 * npar, NEGM1 + 2 * npar)
            for cc in range(NCH):
                dg_build(npar * 2 + cc, RSTD1 + 2 * npar + cc)
        ve(lambda e: e.reduce_sum(statT[:, SSUM:SSUM + 1], rT[:, 0, :],
                                  axis=AX.X), rd=[rT], wr=[statT])
        sc(lambda e: e.activation(EminT[:], rT[:, 0, :], AF.Square,
                                  accum_out=statT[:, SSSQ:SSSQ + 1]),
           rd=[rT], wr=[EminT, statT])
        xs_mms(j, 1)
        y_mms(1)
        gating(1)
        hinc_mms(1)
        hstt(1)
        outproj_chunk(j, 1, xv)
        ve(lambda e: e.reduce_sum(statT[:, SSUM + 1:SSUM + 2], rT[:, 1, :],
                                  axis=AX.X), rd=[rT], wr=[statT])
        sc(lambda e: e.activation(EminT[:], rT[:, 1, :], AF.Square,
                                  accum_out=statT[:, SSSQ + 1:SSSQ + 2]),
           rd=[rT], wr=[EminT, statT])
        ln_rstd(RSTD2, NEGM2)
        for c in range(NCH):
            dg_build(4 + c, RSTD2 + c)
        if j + 1 < n_tiles:
            for cc in range(NCH):
                negmr_row(NEGM1 + 2 * npar + cc, R1ROW + (npar * 2 + cc) * 128)
        for c in range(NCH):
            negmr_row(NEGM2 + c, R2ROW + c * 128)
        for c in range(NCH):
            ve(lambda e, c=c: e.tensor_scalar(
                x1tmT[:, c, :], rT[:, c, :],
                statT[:, RSTD2 + c:RSTD2 + c + 1],
                statT[:, NEGM2 + c:NEGM2 + c + 1], ALU.mult, ALU.add),
               rd=[rT, statT], wr=[x1tmT])
        for c in range(NCH):
            for f in range(8):
                pe(lambda e, c=c, f=f: e.matmul(
                    ps(c * 1024 + f * 128, 128), rT[:, c, f * 128:(f + 1) * 128],
                    DgT[:, (4 + c) * 128:(5 + c) * 128], start=True, stop=False),
                   rd=[rT, DgT], wr=[ps(c * 1024 + f * 128, 128)])
                pe(lambda e, c=c, f=f: e.matmul(
                    ps(c * 1024 + f * 128, 128), onesR[0:1, 0:128],
                    row_ap(R2ROW + c * 128), start=False, stop=True),
                   rd=[onesR, rowT], wr=[ps(c * 1024 + f * 128, 128)])
            ve(lambda e, c=c: e.tensor_copy(
                uT[:, :, c * 128:(c + 1) * 128],
                ps(c * 1024, 1024).rearrange("p (a b) -> p a b", a=8)),
               rd=[ps(c * 1024, 1024)], wr=[uT])
        if j + 2 < n_tiles:
            dma("sync", xv[:], x_hbm(j + 2), dX[par])
        for k in range(8):
            buf = k % 2
            for mt in range(2):
                pe(lambda e, mt=mt, k=k, buf=buf: e.matmul(
                    ps(mt * 512, 256), u1B[:, buf, mt * 128:(mt + 1) * 128],
                    uT[:, k, :], start=(k == 0), stop=(k == 7)),
                   rd=[u1B[:, buf], uT], wr=[ps(mt * 512, 256)])
            nxt = 8 * j + k + 3
            if nxt <= 8 * n_tiles:
                dma("gpsimd", u1B[:, buf], u1_hbm((nxt - 1) % 8),
                    dU1[(nxt - 1) % 2])
        ve(lambda e: e.tensor_copy(PT[:, 0, :], ps(0, 256)),
           rd=[ps(0, 256)], wr=[PT])
        ve(lambda e: e.tensor_copy(PT[:, 1, :], ps(512, 256)),
           rd=[ps(512, 256)], wr=[PT])
        def h_mms(m):
            buf = (32 * j + m) % 3
            hoff = 1024 + (m % 2) * 512
            pe(lambda e, buf=buf, hoff=hoff: e.matmul(
                ps(hoff, 256), v1B[:, buf, 0, :], PT[:, 0, :],
                start=True, stop=False),
               rd=[v1B[:, buf], PT], wr=[ps(hoff, 256)])
            pe(lambda e, buf=buf, hoff=hoff: e.matmul(
                ps(hoff, 256), v1B[:, buf, 1, :], PT[:, 1, :],
                start=False, stop=True),
               rd=[v1B[:, buf], PT], wr=[ps(hoff, 256)])
            sc(lambda e, m=m, hoff=hoff: e.activation(
                HsT[m % 2][:], ps(hoff, 256), AF.Gelu_apprx_tanh,
                bias=b1R[:, m:m + 1]),
               rd=[ps(hoff, 256), b1R], wr=[HsT[m % 2]])

        def q_mms(m):
            buf = (32 * j + m) % 3
            for mt in range(2):
                pe(lambda e, m=m, mt=mt, buf=buf: e.matmul(
                    ps(2048 + mt * 512, 256), u2B[:, buf, mt * 128:(mt + 1) * 128],
                    HsT[m % 2][:], start=(m == 0), stop=(m == 31)),
                   rd=[u2B[:, buf], HsT[m % 2]], wr=[ps(2048 + mt * 512, 256)])
            nxtf = 32 * j + m + 4
            if nxtf <= 32 * n_tiles:
                dma("gpsimd", v1B[:, buf], v1_hbm((nxtf - 1) % 32),
                    dV1[(nxtf - 1) % 3])
                dma("gpsimd", u2B[:, buf], u2_hbm((nxtf - 1) % 32),
                    dU2[(nxtf - 1) % 3])

        h_mms(0)
        for m in range(1, 32):
            h_mms(m)
            q_mms(m - 1)
        q_mms(31)
        ve(lambda e: e.tensor_copy(QT[:, 0, :], ps(2048, 256)),
           rd=[ps(2048, 256)], wr=[QT])
        ve(lambda e: e.tensor_copy(QT[:, 1, :], ps(2560, 256)),
           rd=[ps(2560, 256)], wr=[QT])
        for c in range(NCH):
            for k2 in range(2):
                for n2 in range(2):
                    pe(lambda e, c=c, k2=k2, n2=n2: e.matmul(
                        ps(3072 + n2 * 512, 512), QT[:, k2, c * 128:(c + 1) * 128],
                        v2R[:, k2, n2 * 512:(n2 + 1) * 512],
                        start=(k2 == 0), stop=False),
                       rd=[QT, v2R], wr=[ps(3072 + n2 * 512, 512)])
            for n2 in range(2):
                pe(lambda e, n2=n2: e.matmul(
                    ps(3072 + n2 * 512, 512), onesR[0:1, 0:128],
                    b2R[0:1, n2 * 512:(n2 + 1) * 512], start=False, stop=True),
                   rd=[onesR, b2R], wr=[ps(3072 + n2 * 512, 512)])
            ve(lambda e, c=c: e.tensor_add(
                x1tmT[:, c, :], ps(3072, 1024), x1tmT[:, c, :]),
               rd=[ps(3072, 1024), x1tmT], wr=[x1tmT])
        ln_stats(lambda c: x1tmT[:, c, :])
        ln_rstd(RSTD3, NEGM3)
        for c in range(NCH):
            negmr_row(NEGM3 + c, R3ROW + c * 128)
        for c in range(NCH):
            ve(lambda e, c=c: e.tensor_scalar(
                rT[:, c, :], x1tmT[:, c, :],
                statT[:, RSTD3 + c:RSTD3 + c + 1],
                statT[:, NEGM3 + c:NEGM3 + c + 1], ALU.mult, ALU.add),
               rd=[x1tmT, statT], wr=[rT])
        for c in range(NCH):
            dma("sync", AP(dout, j * TILE * D + c * CH * D,
                           [[D, 128], [1, D]]), rT[:, c, :], dOUT)

    fin = dOUT.n

    def final_wait(e, fin=fin):
        e.wait_ge(dOUT.sem, fin)
        return None
    cur["sync"].append(final_wait)

    for bi, blk in enumerate(blocks):
        with nc.Block(f"blk{bi}", no_gpsimd_drain=True) as B:
            for eng in ENGINES:
                fns = blk[eng]
                if not fns:
                    continue
                getattr(B, eng)(lambda E, fns=fns: [f(E) for f in fns])
    ctx.close()
    return nc


def _prep_shared(L0_attn_W_in, L0_attn_b_in, L0_attn_A_log, L0_attn_D_skip,
                 L0_attn_dt_bias, L0_attn_W_out, L0_attn_b_out, L0_attn_ln_gamma,
                 L0_attn_ln_beta, L0_ffn_U1, L0_ffn_V1, L0_ffn_b1, L0_ffn_U2,
                 L0_ffn_V2, L0_ffn_b2):
    W = np.asarray(L0_attn_W_in, np.float32)
    b = np.asarray(L0_attn_b_in, np.float32)
    c = np.ascontiguousarray
    m = {}
    m["wz"] = c(W[:, 0:2048].reshape(8, 128, 16, 128).transpose(2, 1, 0, 3))
    m["wx"] = c(W[:, 2048:4096].reshape(8, 128, 2048))
    m["wbc"] = c(W[:, 4096:4224].reshape(8, 128, 128).transpose(1, 0, 2))
    m["wdt"] = c(W[:, 4224:4232].reshape(8, 128, 8).transpose(1, 0, 2))
    m["wo"] = c(np.asarray(L0_attn_W_out, np.float32).reshape(16, 128, 1024))
    m["u1"] = c(np.asarray(L0_ffn_U1, np.float32).reshape(8, 128, 256))
    m["v1"] = c(np.asarray(L0_ffn_V1, np.float32).reshape(2, 128, 32, 128)
                .transpose(2, 1, 0, 3))
    m["u2"] = c(np.asarray(L0_ffn_U2, np.float32).reshape(32, 128, 256))
    m["v2"] = c(np.asarray(L0_ffn_V2, np.float32).reshape(2, 128, 1024)
                .transpose(1, 0, 2))
    m["bz"] = c(b[0:2048].reshape(16, 128).T)
    m["bx"] = c(b[2048:4096].reshape(1, 2048))
    m["bB"] = c(b[4096:4160].reshape(64, 1))
    m["bC"] = c(b[4160:4224].reshape(64, 1))
    m["bdt"] = c(np.stack([b[4224:4232], np.asarray(L0_attn_dt_bias, np.float32)]))
    m["bo"] = c(np.asarray(L0_attn_b_out, np.float32).reshape(1, 1024))
    m["b1"] = c(np.asarray(L0_ffn_b1, np.float32).reshape(32, 128).T)
    m["b2"] = c(np.asarray(L0_ffn_b2, np.float32).reshape(1, 1024))
    m["gam"] = c(np.asarray(L0_attn_ln_gamma, np.float32).reshape(8, 128).T)
    m["bet"] = c(np.asarray(L0_attn_ln_beta, np.float32).reshape(8, 128).T)
    m["alog"] = c(np.asarray(L0_attn_A_log, np.float32).reshape(1, 8))
    m["dsk"] = c(np.asarray(L0_attn_D_skip, np.float32).reshape(1, 8))
    m["ident"] = np.eye(128, dtype=np.float32)
    m["tri"] = np.triu(np.ones((128, 128), np.float32))
    m["onesc"] = np.ones((128, 128), np.float32)
    return m


def kernel(L0_attn_input_tensor, L0_attn_ln_gamma, L0_attn_ln_beta, L0_attn_W_in,
           L0_attn_b_in, L0_attn_A_log, L0_attn_D_skip, L0_attn_dt_bias,
           L0_attn_W_out, L0_attn_b_out, L0_ffn_U1, L0_ffn_V1, L0_ffn_b1,
           L0_ffn_U2, L0_ffn_V2, L0_ffn_b2):
    x = np.ascontiguousarray(np.asarray(L0_attn_input_tensor, np.float32))
    shared = _prep_shared(L0_attn_W_in, L0_attn_b_in, L0_attn_A_log,
                          L0_attn_D_skip, L0_attn_dt_bias, L0_attn_W_out,
                          L0_attn_b_out, L0_attn_ln_gamma, L0_attn_ln_beta,
                          L0_ffn_U1, L0_ffn_V1, L0_ffn_b1, L0_ffn_U2,
                          L0_ffn_V2, L0_ffn_b2)
    nc = build_nc(SEQ // TILE)
    in_maps = []
    for i in range(8):
        im = dict(shared)
        im["x"] = np.ascontiguousarray(x[i])
        in_maps.append(im)
    res = run_bass_kernel_spmd(nc, in_maps, core_ids=list(range(8)))
    out = np.stack([np.asarray(r["out"], np.float32) for r in res.results])
    return out
